# version 18
# baseline (speedup 1.0000x reference)
import contextlib
import numpy as np
import concourse.bass as bass
import concourse.mybir as mybir
from concourse.bass_utils import run_bass_kernel_spmd

F32 = mybir.dt.float32
BF16 = mybir.dt.bfloat16
AF = mybir.ActivationFunctionType
ALU = mybir.AluOpType
AX = mybir.AxisListType

D = 1024
KC = 8
NH = 16
DH = 64
DMLP = 4096
NE = 8
LN_EPS = 1e-5
COS_EPS = 1e-6
PI = float(np.pi)

ENGINES = ("tensor", "vector", "scalar", "sync", "gpsimd")


class Buf:
    __slots__ = ("w", "r")

    def __init__(self):
        self.w = None
        self.r = []


class Sched:
    def __init__(self):
        self.q = {e: [] for e in ENGINES}
        self.cnt = {}
        self.waited = {e: {} for e in ENGINES}
        self.esem = {}

    def _merge(self, eng, toks):
        m = {}
        for (s, v) in toks:
            if v > m.get(s, -1):
                m[s] = v
        need = []
        for s, v in m.items():
            if self.waited[eng].get(s, -1) < v:
                need.append((s, v))
                self.waited[eng][s] = v
        return need

    def emit(self, eng, fns, reads=(), writes=(), extra=(), sem=None, inc=1):
        if not isinstance(fns, (list, tuple)):
            fns = [fns]
        toks = list(extra)
        for b in reads:
            if b.w:
                toks.append(b.w)
        for b in writes:
            if b.w:
                toks.append(b.w)
            toks.extend(b.r)
        need = self._merge(eng, toks)
        if sem is None:
            sem = self.esem[eng]
        self.cnt[sem] = self.cnt.get(sem, 0) + inc
        tok = (sem, self.cnt[sem])

        def cmd(E, fns=tuple(fns), need=tuple(need), sem=sem, inc=inc):
            for s, v in need:
                E.wait_ge(s, v)
            ins = None
            for f in fns:
                ins = f(E)
            ins.then_inc(sem, inc)

        self.q[eng].append(cmd)
        for b in reads:
            b.r.append(tok)
        for b in writes:
            b.w = tok
            b.r = []
        return tok

    def barrier_tokens(self):
        return [(s, v) for s, v in self.cnt.items()]


def build_nc(T, n_cores=8, phases=("f1", "att", "mlp", "f2"), ffn_parts=("a", "cc", "b"), att_stop=99, no_affine=False):
    NT = T // 128
    assert T % 256 == 0
    nseg = 2 if NT >= 8 else 1
    tiles_per_seg = NT // nseg
    qtiles = 2 if tiles_per_seg % 2 == 0 else 1
    nc = bass.Bass("TRN2", target_bir_lowering=False, debug=False)

    dram = {}

    def din(name, shape, dt=F32):
        dram[name] = nc.dram_tensor(name, list(shape), dt, kind="ExternalInput").ap()
        return dram[name]

    x_h = din("x", (T, D))
    wv_h = din("wv", (D, NH * DH))
    wqk_h = din("wqk", (D, 2 * NH))
    wo_h = din("wo", (NH * DH, D))
    w1_h = din("w1", (32, 128, 1024))
    w2_h = din("w2", (DMLP, D))
    b1c_h = din("b1c", (128, 32))
    b2bc_h = din("b2bc", (128, D))
    expw_h = din("expw", (NE, 2, 128, 4096))
    expb_h = din("expb", (NE, D))
    gw_h = din("gw", (128, 8 * 8))
    gb_h = din("gb", (1, 8))
    ln1w_h = din("ln1w", (128, D))
    ln1b_h = din("ln1b", (128, D))
    ln2w_h = din("ln2w", (128, D))
    ln2b_h = din("ln2b", (128, D))
    consts_h = din("consts", (128, 8))
    onesc_h = din("ones_col", (128, 1))
    onescb_h = din("ones_col_b", (128, 1), BF16)
    onesr_h = din("ones_row", (1, 128))
    ones2_h = din("ones2", (2, 128), BF16)
    u128_h = din("u128", (128, 128))
    ident_h = din("ident", (128, 128))

    y_h = nc.dram_tensor("y", [T, D], F32, kind="ExternalOutput").ap()

    cc_bufs = []
    for i in range(2):
        cc_bufs.append((nc.dram_tensor(f"cc_in{i}", [1, 8], F32),
                        nc.dram_tensor(f"cc_out{i}", [1, 8], F32)))

    S = Sched()

    with contextlib.ExitStack() as ctx:
        E = ctx.enter_context

        XRES = E(nc.sbuf_tensor("XRES", [128, NT * 1024], F32))
        arnb_size = max(16 * T, 16384 + 16 * (T // nseg))
        ARNB = E(nc.sbuf_tensor("ARNB", [128, arnb_size], BF16))
        ARNC = E(nc.sbuf_tensor("ARNC", [128, 17920], BF16))
        scr = E(nc.sbuf_tensor("scr", [128, 1024], F32))
        scr2 = E(nc.sbuf_tensor("scr2", [128, 1024], F32))
        _scr2_b16 = scr2[:, :].bitcast(BF16)
        bp2 = (_scr2_b16[0:2, 0:512], _scr2_b16[0:2, 512:1024])
        scrF = E(nc.sbuf_tensor("scrF", [128, 256], F32))
        csin = E(nc.sbuf_tensor("csin", [128, 2080], F32))
        ROWS = E(nc.sbuf_tensor("ROWS", [1, 2080], F32))
        carry = ROWS[0:1, 0:2080]
        _rows_b16 = ROWS[0:1, :].bitcast(BF16)
        tq = E(nc.sbuf_tensor("tq", [128, 32], F32))
        cosqk = E(nc.sbuf_tensor("cosqk", [128, 32], F32))
        sinqk = E(nc.sbuf_tensor("sinqk", [128, 32], F32))
        dsc = E(nc.sbuf_tensor("dsc", [128, 80], F32))
        murs = E(nc.sbuf_tensor("murs", [128, 2 * NT], F32))
        lnst = E(nc.sbuf_tensor("lnst", [128, 8], F32))
        gate = E(nc.sbuf_tensor("gate", [128, 8], F32))
        top8 = E(nc.sbuf_tensor("top8", [128, 8], F32))
        vind = E(nc.sbuf_tensor("vind", [128, 8], BF16))
        cntb = E(nc.sbuf_tensor("cntb", [128, 8], F32))
        cntr = E(nc.sbuf_tensor("cntr", [1, 8], F32))
        brow = (ROWS[0:1, 0:512], ROWS[0:1, 1024:1536])
        browh = (_rows_b16[0:1, 1024:1536], _rows_b16[0:1, 3072:3584])
        browl = (_rows_b16[0:1, 1536:2048], _rows_b16[0:1, 3584:4096])
        gbh = E(nc.sbuf_tensor("gbh", [1, 8], BF16))
        gbl = E(nc.sbuf_tensor("gbl", [1, 8], BF16))
        gbr = E(nc.sbuf_tensor("gbr", [1, 8], F32))
        sconst = E(nc.sbuf_tensor("sconst", [128, 8], F32))
        sonesc = E(nc.sbuf_tensor("sonesc", [128, 1], F32))
        sonescb = E(nc.sbuf_tensor("sonescb", [128, 1], BF16))
        sonesr = E(nc.sbuf_tensor("sonesr", [1, 128], F32))
        sones2 = E(nc.sbuf_tensor("sones2", [2, 128], BF16))
        sU = E(nc.sbuf_tensor("sU", [128, 128], F32))
        sI = E(nc.sbuf_tensor("sI", [128, 128], F32))
        gwst = E(nc.sbuf_tensor("gwst", [128, 64], F32))
        gwhi = E(nc.sbuf_tensor("gwhi", [128, 64], BF16))
        gwlo = E(nc.sbuf_tensor("gwlo", [128, 64], BF16))
        BC = E(nc.sbuf_tensor("BC", [128, 4 * D], F32))
        sln1w = BC[:, 0:D]
        sln1b = BC[:, D:2 * D]
        sln2w = BC[:, 2 * D:3 * D]
        sln2b = BC[:, 3 * D:4 * D]
        sb2bc = BC[:, 0:D]
        sb1c = E(nc.sbuf_tensor("sb1c", [128, 32], F32))

        cnt_scale = 0.5 / float(n_cores * T * 2)
        CONST_VALS = [0.0, 1.0, PI / 2, PI / 4, LN_EPS, COS_EPS, 1.0 / D, cnt_scale]
        for j, v in enumerate(CONST_VALS):
            nc.const_aps.register(F32, float(v), sconst[:, j:j + 1])

        PB = [E(nc.psum_tensor(f"PB{i}", [128, 512], F32)) for i in range(8)]

        stn = E(nc.semaphore("stn"))
        vec = E(nc.semaphore("vec"))
        scl = E(nc.semaphore("scl"))
        gps = E(nc.semaphore("gps"))
        dIN = E(nc.semaphore("dIN"))
        dX = [E(nc.semaphore(f"dX{i}")) for i in range(4)]
        dE0 = E(nc.semaphore("dE0"))
        dE1 = E(nc.semaphore("dE1"))
        dB = (E(nc.semaphore("dB0")), E(nc.semaphore("dB1")))
        dBc = (E(nc.semaphore("dBc0")), E(nc.semaphore("dBc1")))
        dWV = E(nc.semaphore("dWV"))
        dWVs = E(nc.semaphore("dWVs"))
        dWOs = E(nc.semaphore("dWOs"))
        dWO = E(nc.semaphore("dWO"))
        dWQ = E(nc.semaphore("dWQ"))
        dW1 = E(nc.semaphore("dW1"))
        dW2 = E(nc.semaphore("dW2"))
        dST = E(nc.semaphore("dST"))
        dCC = E(nc.semaphore("dCC"))
        dC0 = E(nc.semaphore("dC0"))
        dC1 = E(nc.semaphore("dC1"))
        dWVc = E(nc.semaphore("dWVc"))
        dWOc = E(nc.semaphore("dWOc"))
        dWQc = E(nc.semaphore("dWQc"))
        dW1c = (E(nc.semaphore("dW1c0")), E(nc.semaphore("dW1c1")))
        dW2c = (E(nc.semaphore("dW2c0")), E(nc.semaphore("dW2c1")))
        dW1s = E(nc.semaphore("dW1s"))
        dW2s = E(nc.semaphore("dW2s"))
        dBC = E(nc.semaphore("dBC"))
        dCR = E(nc.semaphore("dCR"))
        S.esem.update(tensor=stn, vector=vec, scalar=scl, gpsimd=gps, sync=None)

        bx = [Buf() for _ in range(NT)]
        bscr, bscr2, bscrF = Buf(), Buf(), Buf()
        bbp2 = (Buf(), Buf())
        bcsin, bcarry = Buf(), Buf()
        btq, bcos, bsin, bdsc = Buf(), Buf(), Buf(), Buf()
        bmurs = Buf()
        blnst = Buf()
        bgate, btop8, bvind, bcntb, bcntr = Buf(), Buf(), Buf(), Buf(), Buf()
        bbrow = (Buf(), Buf())
        bbrowh = (Buf(), Buf())
        bbrowl = (Buf(), Buf())
        bgb = Buf()
        bgw = Buf()
        bconsts = Buf()
        bpb = [Buf() for _ in range(8)]
        barnb = [Buf() for _ in range(NT)]
        barnc = [Buf() for _ in range(8)]
        brole = [Buf() for _ in range(4)]
        bsln = Buf()
        bsb = Buf()

        arnb_f32 = ARNB[:, :].bitcast(F32)
        arnc_f32 = ARNC[:, :].bitcast(F32)

        def arena_turnover(bufs):
            toks = []
            for b in bufs:
                if b.w:
                    toks.append(b.w)
                toks.extend(b.r)
                b.w = None
                b.r = []
            return toks

        em = S.emit

        ng = max(1, NT // 4)
        for t in range(NT):
            em("sync", (lambda E, t=t: E.dma_start(
                out=XRES[:, t * 1024:(t + 1) * 1024], in_=x_h[t * 128:(t + 1) * 128, :])),
                writes=[bx[t]], sem=dX[t // ng], inc=16)
        for g in range(4):
            lo, hi = g * ng, min((g + 1) * ng, NT)
            if lo >= NT:
                break
            tok = (dX[g], S.cnt[dX[g]])
            for t in range(lo, hi):
                bx[t].w = tok
        for hb, sb, b in ((consts_h, sconst, bconsts), (onesc_h, sonesc, bconsts),
                          (onescb_h, sonescb, bconsts), (onesr_h, sonesr, bconsts),
                          (ones2_h, sones2, bconsts),
                          (u128_h, sU, bconsts), (ident_h, sI, bconsts),
                          (gw_h, gwst, bgw), (gb_h, gbr, bgb),
                          (b1c_h, sb1c, bsb)):
            em("sync", (lambda E, hb=hb, sb=sb: E.dma_start(out=sb[:, :], in_=hb[:, :])),
               writes=[b], sem=dIN, inc=16)
        _init_total = (dIN, S.cnt[dIN])
        for b in [bconsts, bgw, bgb, bsln, bsb]:
            b.w = _init_total

        em("scalar", (lambda E: E.copy(out=gwhi[:, :], in_=gwst[:, :])), reads=[bgw], writes=[bgw])
        em("vector", (lambda E: E.tensor_sub(gwlo[:, :], gwst[:, :], gwhi[:, :])),
           reads=[bgw], writes=[bgw])
        em("scalar", (lambda E: E.copy(out=gbh[:, :], in_=gbr[:, :])), reads=[bgb], writes=[bgb])
        em("vector", (lambda E: E.tensor_sub(gbl[:, :], gbr[:, :], gbh[:, :])),
           reads=[bgb], writes=[bgb])

        def ln_stats(src_ap, src_bufs, mu_slot, r_slot, dst_buf, keep_vpe=None,
                     sq=None, sq_buf=None):
            s1 = lnst[:, mu_slot:mu_slot + 1]
            s2 = lnst[:, r_slot:r_slot + 1]
            t3 = lnst[:, 6:7]
            t4 = lnst[:, 7:8]
            em("vector", (lambda E: E.reduce_sum(s1, src_ap, axis=AX.X)),
               reads=src_bufs, writes=[dst_buf])
            if sq is None:
                sq, sq_buf = scr2[:, :], bscr2
            em("scalar", (lambda E, sq=sq: E.activation(out=sq, in_=src_ap, func=AF.Square)),
               reads=src_bufs, writes=[sq_buf])
            em("vector", (lambda E, sq=sq: E.reduce_sum(s2, sq, axis=AX.X)),
               reads=[sq_buf], writes=[dst_buf])
            em("vector", (lambda E: E.tensor_scalar_mul(s1, s1, 1.0 / D)),
               reads=[dst_buf], writes=[dst_buf])
            em("vector", (lambda E: E.tensor_scalar_mul(s2, s2, 1.0 / D)),
               reads=[dst_buf], writes=[dst_buf])
            em("vector", (lambda E: E.tensor_mul(t3, s1, s1)), reads=[dst_buf], writes=[dst_buf])
            em("vector", (lambda E: E.tensor_sub(s2, s2, t3)), reads=[dst_buf], writes=[dst_buf])
            em("vector", (lambda E: E.tensor_scalar_add(s2, s2, LN_EPS)),
               reads=[dst_buf], writes=[dst_buf])
            if keep_vpe is not None:
                em("vector", (lambda E: E.tensor_copy(lnst[:, keep_vpe:keep_vpe + 1], s2)),
                   reads=[dst_buf], writes=[dst_buf])
            em("scalar", (lambda E: E.activation(out=t3, in_=s2, func=AF.Sqrt)),
               reads=[dst_buf], writes=[dst_buf])
            em("vector", (lambda E: E.reciprocal(t4, t3)), reads=[dst_buf], writes=[dst_buf])
            em("vector", (lambda E: E.tensor_mul(t3, t4, t4)), reads=[dst_buf], writes=[dst_buf])
            em("vector", (lambda E: E.tensor_mul(t3, t3, s2)), reads=[dst_buf], writes=[dst_buf])
            em("vector", (lambda E: E.tensor_scalar(t3, t3, -0.5, 1.5, ALU.mult, ALU.add)),
               reads=[dst_buf], writes=[dst_buf])
            em("vector", (lambda E: E.tensor_mul(s2, t4, t3)), reads=[dst_buf], writes=[dst_buf])

        def normalize(dst_ap, dst_buf, src_ap, src_bufs, mu_slot, r_slot, stats_buf):
            em("vector", (lambda E: E.tensor_scalar(
                dst_ap, src_ap, lnst[:, mu_slot:mu_slot + 1], lnst[:, r_slot:r_slot + 1],
                ALU.subtract, ALU.mult)),
               reads=list(src_bufs) + [stats_buf], writes=[dst_buf])

        def transpose_split(src, src_buf, dst_hi_mk, dst_lo_mk, dst_bufs,
                            affine=None, banks=(0, 1)):
            for g in range(2):
                bk = banks[g]
                pb = PB[bk]
                fns = []
                for c4 in range(4):
                    c = g * 4 + c4
                    fns.append(lambda E, c=c, c4=c4, pb=pb: E.transpose(
                        pb[:, c4 * 128:(c4 + 1) * 128], src[:, c * 128:(c + 1) * 128], sI[:, :]))
                ttok = em("tensor", fns, reads=[src_buf, bconsts], writes=[bpb[bk]])
                if affine is None:
                    hi_ap = dst_hi_mk(g)
                    lo_ap = dst_lo_mk(g)
                    em("scalar", (lambda E, pb=pb, hi_ap=hi_ap: E.copy(out=hi_ap, in_=pb[:, :])),
                       reads=[bpb[bk]], writes=dst_bufs)
                    em("vector", (lambda E, pb=pb, hi_ap=hi_ap, lo_ap=lo_ap: E.tensor_tensor(
                        lo_ap, pb[:, :].rearrange("p (c n) -> p c n", c=4), hi_ap, ALU.subtract)),
                       reads=[bpb[bk]] + list(dst_bufs), writes=dst_bufs)
                else:
                    wc, bc, abuf = affine
                    for c4 in range(4):
                        c = g * 4 + c4
                        sl = pb[:, c4 * 128:(c4 + 1) * 128]
                        hi_ap = dst_hi_mk(c)
                        lo_ap = dst_lo_mk(c)
                        em("scalar", (lambda E, sl=sl, hi_ap=hi_ap, c=c, wc=wc, bc=bc:
                                      E.activation(out=hi_ap, in_=sl, func=AF.Identity,
                                                   bias=bc[:, c:c + 1], scale=wc[:, c:c + 1])),
                           reads=[bpb[g], abuf], writes=dst_bufs)
                        em("vector", (lambda E, sl=sl, c=c, wc=wc, bc=bc: E.scalar_tensor_tensor(
                            scrF[:, 0:128], sl, wc[:, c:c + 1],
                            bc[:, c:c + 1].broadcast_to([128, 128]),
                            ALU.mult, ALU.add)),
                           reads=[bpb[g], abuf], writes=[bscrF])
                        em("vector", (lambda E, hi_ap=hi_ap, lo_ap=lo_ap: E.tensor_tensor(
                            lo_ap, scrF[:, 0:128], hi_ap, ALU.subtract)),
                           reads=[bscrF] + list(dst_bufs), writes=dst_bufs)

        def mm3_group(pb_ap, lhs_hi, lhs_lo, rhs_hi, rhs_lo, first, last):
            return [
                (lambda E: E.matmul(pb_ap, lhs_hi, rhs_hi, start=first, stop=False)),
                (lambda E: E.matmul(pb_ap, lhs_hi, rhs_lo, start=False, stop=False)),
                (lambda E: E.matmul(pb_ap, lhs_lo, rhs_hi, start=False, stop=last)),
            ]

        def xnT_hi(c):
            return ARNB[:, c * T:(c + 1) * T]

        def xnT_lo(c):
            return ARNB[:, NT * 1024 + c * T: NT * 1024 + (c + 1) * T]

        arnb_r = ARNB[:, 0:16 * T].rearrange("p (c n) -> p c n", c=2 * KC)

        def ffn_phase(call_idx):
            cc_in, cc_out = cc_bufs[call_idx]
            do_a = "a" in ffn_parts
            do_cc = "cc" in ffn_parts
            do_b = "b" in ffn_parts
            pa_buf = (scr, scr2)
            pa_bbuf = (bscr, bscr2)

            def pa_s1(t):
                xt = XRES[:, t * 1024:(t + 1) * 1024]
                b = pa_buf[t % 2]
                bb = pa_bbuf[t % 2]
                ln_stats(xt, [bx[t]], 0, 1, blnst, sq=b[:, :], sq_buf=bb)
                normalize(b[:, :], bb, xt, [bx[t]], 0, 1, blnst)

            def pa_s2(t):
                b = pa_buf[t % 2]
                bb = pa_bbuf[t % 2]
                transpose_split(
                    b[:, :], bb,
                    dst_hi_mk=lambda g, t=t: arnb_r[:, g * 4:(g + 1) * 4, t * 128:(t + 1) * 128],
                    dst_lo_mk=lambda g, t=t: arnb_r[:, KC + g * 4:KC + (g + 1) * 4, t * 128:(t + 1) * 128],
                    dst_bufs=[barnb[t]])
                gpb = PB[2 + (t % 2)]
                gbuf = bpb[2 + (t % 2)]
                fns = []
                for k in range(KC):
                    lh = xnT_hi(k)[:, t * 128:(t + 1) * 128]
                    ll = xnT_lo(k)[:, t * 128:(t + 1) * 128]
                    fns += mm3_group(gpb[:, 0:8], lh, ll,
                                     gwhi[:, k * 8:(k + 1) * 8], gwlo[:, k * 8:(k + 1) * 8],
                                     first=(k == 0), last=False)
                fns.append(lambda E, gpb=gpb: E.matmul(gpb[:, 0:8], sones2[0:1, :], gbh[0:1, :],
                                                       start=False, stop=False))
                fns.append(lambda E, gpb=gpb: E.matmul(gpb[:, 0:8], sones2[0:1, :], gbl[0:1, :],
                                                       start=False, stop=True))
                em("tensor", fns, reads=[barnb[t], bgw, bgb, bconsts], writes=[gbuf])

            def pa_s3(t):
                gpb = PB[2 + (t % 2)]
                gbuf = bpb[2 + (t % 2)]
                em("vector", (lambda E, gpb=gpb: E.tensor_copy(gate[:, :], gpb[:, 0:8])),
                   reads=[gbuf], writes=[bgate])
                em("vector", (lambda E: E.max(top8[:, :], gate[:, :])),
                   reads=[bgate], writes=[btop8])
                em("vector", (lambda E: E.tensor_single_scalar(
                    vind[:, :], gate[:, :], top8[:, 1:2], ALU.is_ge)),
                   reads=[bgate, btop8], writes=[bvind])
                em("tensor", (lambda E, t=t: E.matmul(PB[4][0:1, 0:8], sonescb[:, :], vind[:, :],
                                                      start=(t == 0), stop=(t == NT - 1))),
                   reads=[bvind, bconsts],
                   writes=([bpb[4]] if t in (0, NT - 1) else []))

            if do_a:
                pa_s1(0)
                for t in range(NT):
                    pa_s2(t)
                    if t + 1 < NT:
                        pa_s1(t + 1)
                    pa_s3(t)
            if not (do_a and do_cc and do_b):
                return
            em("scalar", (lambda E: E.copy(out=cntr[:, :], in_=PB[4][0:1, 0:8])),
               reads=[bpb[4]], writes=[bcntr])
            em("gpsimd", (lambda E: E.dma_start(out=cc_in[:, :], in_=cntr[:, :])),
               reads=[bcntr], sem=dCC, inc=16)
            cc_tok = em("gpsimd", (lambda E: E.collective_compute(
                "AllReduce", ALU.add, replica_groups=[list(range(8))],
                ins=[cc_in[:, :]], outs=[cc_out[:, :]])),
               extra=[(dCC, S.cnt.get(dCC, 0))])
            em("gpsimd", (lambda E: E.dma_start(out=cntr[:, :], in_=cc_out[:, :])),
               writes=[bcntr], extra=[cc_tok], sem=dCC, inc=16)
            em("tensor", (lambda E: E.matmul(PB[5][:, 0:8], sonesr[0:1, :], cntr[0:1, :],
                                             start=True, stop=True)),
               reads=[bcntr, bconsts], writes=[bpb[5]])
            em("scalar", (lambda E: E.activation(out=cntb[:, :], in_=PB[5][:, 0:8],
                                                 func=AF.Identity, scale=cnt_scale)),
               reads=[bpb[5]], writes=[bcntb])

            stage_f = arnc_f32[:, 0:512]
            b_stage = barnc[0]

            def wem_hi(s):
                return ARNC[:, 1024 + s * 8192: 1024 + s * 8192 + 4096]

            def wem_lo(s):
                return ARNC[:, 5120 + s * 8192: 5120 + s * 8192 + 4096]

            b_wem = [barnc[1], barnc[2]]

            dE = (dE0, dE1)

            def load_expert(e, half, s):
                em("gpsimd", (lambda E, e=e, half=half, s=s: E.dma_start(
                    out=wem_hi(s), in_=expw_h[e, half, :, :])),
                   writes=[b_wem[s]], sem=(dC0, dC1)[s], inc=16)
                for wave in range(8):
                    em("sync", (lambda E, e=e, half=half, wave=wave: E.dma_start(
                        out=stage_f[:, :],
                        in_=expw_h[e, half, :, wave * 512:(wave + 1) * 512])),
                       writes=[b_stage], sem=dE[s], inc=16)
                    em("vector", (lambda E, s=s, wave=wave: E.tensor_tensor(
                        wem_lo(s)[:, wave * 512:(wave + 1) * 512],
                        stage_f[:, :],
                        wem_hi(s)[:, wave * 512:(wave + 1) * 512], ALU.subtract)),
                       reads=[b_stage, b_wem[s]], writes=[b_wem[s], b_stage])
                em("sync", (lambda E, e=e, half=half, s=s: E.dma_start(
                    out=brow[s], in_=expb_h[e:e + 1, half * 512:(half + 1) * 512])),
                   writes=[bbrow[s]], sem=dB[s], inc=16)
                em("scalar", (lambda E, s=s: E.copy(out=bp2[s][0:1, :], in_=brow[s])),
                   reads=[bbrow[s]], writes=[bbp2[s]])
                em("vector", (lambda E, s=s: E.tensor_sub(browl[s], brow[s], bp2[s][0:1, :])),
                   reads=[bbrow[s], bbp2[s]], writes=[bbrowl[s]])
                em("gpsimd", (lambda E, s=s: E.dma_start(out=bp2[s][1:2, :], in_=browl[s])),
                   reads=[bbrowl[s]], writes=[bbp2[s]], sem=dBc[s], inc=16)

            def load_expert_cast(e, half, s):
                em("gpsimd", (lambda E, e=e, half=half, s=s: E.dma_start(
                    out=wem_hi(s), in_=expw_h[e, half, :, :])),
                   writes=[b_wem[s]], sem=(dC0, dC1)[s], inc=16)

            def load_expert_wave(e, half, s, wave):
                em("sync", (lambda E, e=e, half=half, wave=wave: E.dma_start(
                    out=stage_f[:, :],
                    in_=expw_h[e, half, :, wave * 512:(wave + 1) * 512])),
                   writes=[b_stage], sem=dE[s], inc=16)
                em("vector", (lambda E, s=s, wave=wave: E.tensor_tensor(
                    wem_lo(s)[:, wave * 512:(wave + 1) * 512],
                    stage_f[:, :],
                    wem_hi(s)[:, wave * 512:(wave + 1) * 512], ALU.subtract)),
                   reads=[b_stage, b_wem[s]], writes=[b_wem[s], b_stage])

            def load_expert_bias(e, half, s):
                em("sync", (lambda E, e=e, half=half, s=s: E.dma_start(
                    out=brow[s], in_=expb_h[e:e + 1, half * 512:(half + 1) * 512])),
                   writes=[bbrow[s]], sem=dB[s], inc=16)
                em("scalar", (lambda E, s=s: E.copy(out=bp2[s][0:1, :], in_=brow[s])),
                   reads=[bbrow[s]], writes=[bbp2[s]])
                em("vector", (lambda E, s=s: E.tensor_sub(browl[s], brow[s], bp2[s][0:1, :])),
                   reads=[bbrow[s], bbp2[s]], writes=[bbrowl[s]])
                em("gpsimd", (lambda E, s=s: E.dma_start(out=bp2[s][1:2, :], in_=browl[s])),
                   reads=[bbrowl[s]], writes=[bbp2[s]], sem=dBc[s], inc=16)

            for b in bbp2:
                if bscr2.w:
                    b.r.append(bscr2.w)
                b.r.extend(bscr2.r)
            seq = [(e, h) for e in range(NE) for h in range(2)]
            load_expert(*seq[0], 0)
            for i, (e, half) in enumerate(seq):
                s = i % 2
                if i + 1 < len(seq):
                    load_expert_cast(*seq[i + 1], 1 - s)
                for t in range(NT):
                    if i + 1 < len(seq):
                        for wv in range(t * 8 // NT, (t + 1) * 8 // NT):
                            load_expert_wave(*seq[i + 1], 1 - s, wv)
                        if t == min(1, NT - 1):
                            load_expert_bias(*seq[i + 1], 1 - s)
                    pb = PB[t % 4]
                    pbuf = bpb[t % 4]
                    fns = []
                    for k in range(KC):
                        lh = xnT_hi(k)[:, t * 128:(t + 1) * 128]
                        ll = xnT_lo(k)[:, t * 128:(t + 1) * 128]
                        fns += mm3_group(pb[:, :], lh, ll,
                                         wem_hi(s)[:, k * 512:(k + 1) * 512],
                                         wem_lo(s)[:, k * 512:(k + 1) * 512],
                                         first=(k == 0), last=False)
                    fns.append(lambda E, pb=pb, s=s: E.matmul(pb[:, :], sones2[:, :], bp2[s][:, :],
                                                              start=False, stop=True))
                    em("tensor", fns,
                       reads=[barnb[t], b_wem[s], bbp2[s], bconsts], writes=[pbuf])
                    half_ap = scr[:, (t % 2) * 512:(t % 2) * 512 + 512]
                    em("scalar", (lambda E, pb=pb, half_ap=half_ap, e=e: E.activation(
                        out=half_ap, in_=pb[:, :], func=AF.Relu, scale=cntb[:, e:e + 1])),
                       reads=[pbuf, bcntb], writes=[bscr])
                    em("vector", (lambda E, t=t, half=half, half_ap=half_ap: E.tensor_add(
                        XRES[:, t * 1024 + half * 512: t * 1024 + half * 512 + 512],
                        XRES[:, t * 1024 + half * 512: t * 1024 + half * 512 + 512],
                        half_ap)),
                       reads=[bscr], writes=[bx[t]])

        seg_T = T // nseg

        def _bp2_handoff():
            toks = arena_turnover(list(bbp2))
            bscr2.r.extend(toks)

        def attn_mlp_segment(seg):
            do_att = "att" in phases
            do_mlp = "mlp" in phases
            t0 = seg * tiles_per_seg
            wo_hi = ARNC[:, 0:8192]
            wo_lo = ARNC[:, 8192:16384]
            stg = arnc_f32[:, 8192:8704]
            wqk_hi = ARNC[:, 17408:17664]
            wqk_lo = ARNC[:, 17664:17920]
            b_wo, b_stg, b_wqk = barnc[3], barnc[4], barnc[5]
            wv_hi = ARNB[:, 0:8192]
            wv_lo = ARNB[:, 8192:16384]
            b_wv = brole[0]
            stgB_att = ARNB[:, 16384:17408].bitcast(F32)
            b_stgB_att = brole[3]

            def y2T_hi(c):
                return ARNB[:, 16384 + c * seg_T: 16384 + (c + 1) * seg_T]

            def y2T_lo(c):
                return ARNB[:, 16384 + KC * seg_T + c * seg_T: 16384 + KC * seg_T + (c + 1) * seg_T]

            arnb_y2_hi_r = ARNB[:, 16384:16384 + KC * seg_T].rearrange("p (c n) -> p c n", c=KC)
            arnb_y2_lo_r = ARNB[:, 16384 + KC * seg_T:16384 + 2 * KC * seg_T].rearrange("p (c n) -> p c n", c=KC)

            b_y2T = brole[1]

            em("gpsimd", (lambda E: E.dma_start(
                out=wv_hi.rearrange("p (c n) -> p c n", c=KC),
                in_=wv_h[:, :].rearrange("(c p) n -> p c n", p=128))),
               writes=[b_wv], sem=dWVc, inc=16)
            for k in range(KC):
                for wv_w, stq, stb, q_eng, q_sem in ((0, stg, b_stg, "sync", dWV),
                                                     (1, None, None, "scalar", dWVs)):
                    if wv_w == 1:
                        stq, stb = stgB_att, b_stgB_att
                    em(q_eng, (lambda E, k=k, wv_w=wv_w, stq=stq: E.dma_start(
                        out=stq, in_=wv_h[k * 128:(k + 1) * 128, wv_w * 512:(wv_w + 1) * 512])),
                       writes=[stb], sem=q_sem, inc=16)
                    em("vector", (lambda E, k=k, wv_w=wv_w, stq=stq: E.tensor_tensor(
                        wv_lo[:, k * 1024 + wv_w * 512:k * 1024 + (wv_w + 1) * 512], stq,
                        wv_hi[:, k * 1024 + wv_w * 512:k * 1024 + (wv_w + 1) * 512], ALU.subtract)),
                       reads=[stb, b_wv], writes=[b_wv, stb])
            em("gpsimd", (lambda E: E.dma_start(
                out=wo_hi.rearrange("p (c n) -> p c n", c=KC),
                in_=wo_h[:, :].rearrange("(c p) n -> p c n", p=128))),
               writes=[b_wo], sem=dWOc, inc=16)
            for k in range(KC):
                for wo_w in range(2):
                    stq, stb = (stg, b_stg) if wo_w == 0 else (stgB_att, b_stgB_att)
                    q_eng, q_sem = ("sync", dWO) if wo_w == 0 else ("scalar", dWOs)
                    em(q_eng, (lambda E, k=k, wo_w=wo_w, stq=stq: E.dma_start(
                        out=stq, in_=wo_h[k * 128:(k + 1) * 128, wo_w * 512:(wo_w + 1) * 512])),
                       writes=[stb], sem=q_sem, inc=16)
                    em("vector", (lambda E, k=k, wo_w=wo_w, stq=stq: E.tensor_tensor(
                        wo_lo[:, k * 1024 + wo_w * 512:k * 1024 + (wo_w + 1) * 512], stq,
                        wo_hi[:, k * 1024 + wo_w * 512:k * 1024 + (wo_w + 1) * 512], ALU.subtract)),
                       reads=[stb, b_wo], writes=[b_wo, stb])
            em("gpsimd", (lambda E: E.dma_start(
                out=wqk_hi.rearrange("p (c n) -> p c n", c=KC),
                in_=wqk_h[:, :].rearrange("(c p) n -> p c n", p=128))),
               writes=[b_wqk], sem=dWQc, inc=16)
            em("sync", (lambda E: E.dma_start(
                out=stg[:, 0:256].rearrange("p (c n) -> p c n", c=KC),
                in_=wqk_h[:, :].rearrange("(c p) n -> p c n", p=128))),
               writes=[b_stg], sem=dWQ, inc=16)
            em("vector", (lambda E: E.tensor_tensor(wqk_lo, stg[:, 0:256], wqk_hi, ALU.subtract)),
               reads=[b_stg, b_wqk], writes=[b_wqk, b_stg])
            for bc_h, bc_dst in ((ln1w_h, sln1w), (ln1b_h, sln1b),
                                 (ln2w_h, sln2w), (ln2b_h, sln2b)):
                em("sync", (lambda E, bc_h=bc_h, bc_dst=bc_dst: E.dma_start(
                    out=bc_dst, in_=bc_h[:, :])), writes=[bsln], sem=dBC, inc=16)
            _stgB_toks = arena_turnover([b_stgB_att])
            b_y2T.r.extend(_stgB_toks)

            scr2_b = scr2[:, :].bitcast(BF16)
            yT_hi = scr2_b[:, 0:1024]
            yT_lo = scr2_b[:, 1024:2048]
            b_yT = bscr2

            hT_hi = csin[:, :].bitcast(BF16)[:, 2048:3072]
            hT_lo = csin[:, :].bitcast(BF16)[:, 3072:4096]

            def front(tt):
                t = t0 + tt
                xt = XRES[:, t * 1024:(t + 1) * 1024]
                ln_stats(xt, [bx[t]], 0, 1, blnst, keep_vpe=2)
                em("vector", (lambda E, t=t: E.tensor_copy(murs[:, 2 * t:2 * t + 2], lnst[:, 0:2])),
                   reads=[blnst], writes=[bmurs])
                normalize(scr[:, :], bscr, xt, [bx[t]], 0, 1, blnst)
                t6 = lnst[:, 6:7]
                t7 = lnst[:, 7:8]
                r1 = lnst[:, 3:4]
                em("vector", (lambda E: E.tensor_mul(t6, lnst[:, 1:2], lnst[:, 1:2])),
                   reads=[blnst], writes=[blnst])
                em("vector", (lambda E: E.tensor_scalar_add(t7, lnst[:, 2:3], -LN_EPS)),
                   reads=[blnst], writes=[blnst])
                em("vector", (lambda E: E.tensor_mul(t6, t6, t7)),
                   reads=[blnst], writes=[blnst])
                em("vector", (lambda E: E.tensor_scalar_add(t6, t6, LN_EPS)),
                   reads=[blnst], writes=[blnst])
                em("scalar", (lambda E: E.activation(out=t7, in_=t6, func=AF.Sqrt)),
                   reads=[blnst], writes=[blnst])
                em("vector", (lambda E: E.reciprocal(r1, t7)),
                   reads=[blnst], writes=[blnst])
                em("vector", (lambda E: E.tensor_mul(t7, r1, r1)),
                   reads=[blnst], writes=[blnst])
                em("vector", (lambda E: E.tensor_mul(t7, t7, t6)),
                   reads=[blnst], writes=[blnst])
                em("vector", (lambda E: E.tensor_scalar(t7, t7, -0.5, 1.5, ALU.mult, ALU.add)),
                   reads=[blnst], writes=[blnst])
                em("vector", (lambda E: E.tensor_mul(r1, r1, t7)),
                   reads=[blnst], writes=[blnst])
                em("vector", (lambda E, r1=r1: E.tensor_scalar_mul(scr[:, :], scr[:, :], r1)),
                   reads=[blnst], writes=[bscr])
                em("vector", (lambda E: E.tensor_mul(scr[:, :], scr[:, :], sln1w)),
                   reads=[bsln], writes=[bscr])
                em("vector", (lambda E: E.tensor_add(scr[:, :], scr[:, :], sln1b)),
                   reads=[bsln], writes=[bscr])
                transpose_split(
                    scr[:, :], bscr,
                    dst_hi_mk=lambda g: yT_hi[:, g * 512:(g + 1) * 512].rearrange("p (c n) -> p c n", c=4),
                    dst_lo_mk=lambda g: yT_lo[:, g * 512:(g + 1) * 512].rearrange("p (c n) -> p c n", c=4),
                    dst_bufs=[b_yT])
                fns0, fns1, fns2 = [], [], []
                for k in range(KC):
                    lh = yT_hi[:, k * 128:(k + 1) * 128]
                    ll = yT_lo[:, k * 128:(k + 1) * 128]
                    fns0 += mm3_group(PB[0][:, :], lh, ll,
                                      wv_hi[:, k * 1024:k * 1024 + 512],
                                      wv_lo[:, k * 1024:k * 1024 + 512],
                                      first=(k == 0), last=(k == KC - 1))
                    fns1 += mm3_group(PB[1][:, :], lh, ll,
                                      wv_hi[:, k * 1024 + 512:(k + 1) * 1024],
                                      wv_lo[:, k * 1024 + 512:(k + 1) * 1024],
                                      first=(k == 0), last=(k == KC - 1))
                    fns2 += mm3_group(PB[7][:, 64:96], lh, ll,
                                      wqk_hi[:, k * 32:(k + 1) * 32],
                                      wqk_lo[:, k * 32:(k + 1) * 32],
                                      first=(k == 0), last=(k == KC - 1))
                em("tensor", fns0, reads=[b_yT, b_wv], writes=[bpb[0]])
                em("tensor", fns1, reads=[b_yT, b_wv], writes=[bpb[1]])
                em("tensor", fns2, reads=[b_yT, b_wqk], writes=[bpb[7]])

            def back1(tt):
                em("scalar", (lambda E: E.activation(out=tq[:, :], in_=PB[7][:, 64:96], func=AF.Tanh)),
                   reads=[bpb[7]], writes=[btq])
                em("scalar", (lambda E: E.activation(out=cosqk[:, :], in_=tq[:, :], func=AF.Sin,
                                                     bias=PI / 2, scale=PI / 4)),
                   reads=[btq], writes=[bcos])
                em("scalar", (lambda E: E.activation(out=sinqk[:, :], in_=tq[:, :], func=AF.Sin,
                                                     bias=0.0, scale=PI / 4)),
                   reads=[btq], writes=[bsin])
                em("vector", (lambda E: E.tensor_tensor(
                    csin[:, 0:512].rearrange("p (i h) -> p i h", i=8),
                    PB[0][:, :].rearrange("p (i h) -> p i h", i=8),
                    cosqk[:, 16:24].broadcast_to([128, 8, 64]), ALU.mult)),
                   reads=[bpb[0], bcos], writes=[bcsin])
                em("vector", (lambda E: E.tensor_tensor(
                    csin[:, 512:1024].rearrange("p (i h) -> p i h", i=8),
                    PB[1][:, :].rearrange("p (i h) -> p i h", i=8),
                    cosqk[:, 24:32].broadcast_to([128, 8, 64]), ALU.mult)),
                   reads=[bpb[1], bcos], writes=[bcsin])
                em("vector", (lambda E: E.tensor_tensor(
                    csin[:, 1024:1536].rearrange("p (i h) -> p i h", i=8),
                    PB[0][:, :].rearrange("p (i h) -> p i h", i=8),
                    sinqk[:, 16:24].broadcast_to([128, 8, 64]), ALU.mult)),
                   reads=[bpb[0], bsin], writes=[bcsin])
                em("vector", (lambda E: E.tensor_tensor(
                    csin[:, 1536:2048].rearrange("p (i h) -> p i h", i=8),
                    PB[1][:, :].rearrange("p (i h) -> p i h", i=8),
                    sinqk[:, 24:32].broadcast_to([128, 8, 64]), ALU.mult)),
                   reads=[bpb[1], bsin], writes=[bcsin])
                em("vector", (lambda E: E.tensor_copy(csin[:, 2048:2064], cosqk[:, 16:32])),
                   reads=[bcos], writes=[bcsin])
                em("vector", (lambda E: E.tensor_copy(csin[:, 2064:2080], sinqk[:, 16:32])),
                   reads=[bsin], writes=[bcsin])

            def back2(tt):
                t = t0 + tt
                xt = XRES[:, t * 1024:(t + 1) * 1024]
                groups = [(0, 512, PB[3], 3), (512, 512, PB[4], 4),
                          (1024, 512, PB[5], 5), (1536, 512, PB[6], 6),
                          (2048, 32, PB[7], 7)]
                for (off, w, pbk, pbi) in groups:
                    fns = []
                    if t > 0:
                        fns.append(lambda E, pbk=pbk, off=off, w=w: E.matmul(
                            pbk[:, 0:w], sonesr[0:1, :], carry[0:1, off:off + w],
                            start=True, stop=False))
                    fns.append(lambda E, pbk=pbk, off=off, w=w, t=t: E.matmul(
                        pbk[:, 0:w], sU[:, :], csin[:, off:off + w],
                        start=(t == 0), stop=True))
                    em("tensor", fns, reads=[bcsin, bcarry, bconsts], writes=[bpb[pbi]])
                if t < NT - 1:
                    for (off, w, pbk, pbi) in groups:
                        em("scalar", (lambda E, pbk=pbk, off=off, w=w: E.copy(
                            out=csin[96:128, off:off + w], in_=pbk[96:128, 0:w])),
                           reads=[bpb[pbi]], writes=[bcsin])
                    em("sync", (lambda E: E.dma_start(
                        out=carry[0:1, 0:2080], in_=csin[127:128, 0:2080])),
                       reads=[bcsin], writes=[bcarry], sem=dCR, inc=16)
                em("vector", (lambda E: E.tensor_tensor(dsc[:, 0:16], PB[7][:, 0:16],
                                                        cosqk[:, 0:16], ALU.mult)),
                   reads=[bpb[7], bcos], writes=[bdsc])
                em("vector", (lambda E: E.tensor_tensor(dsc[:, 16:32], PB[7][:, 16:32],
                                                        sinqk[:, 0:16], ALU.mult)),
                   reads=[bpb[7], bsin], writes=[bdsc])
                em("vector", (lambda E: E.tensor_add(dsc[:, 0:16], dsc[:, 0:16], dsc[:, 16:32])),
                   reads=[bdsc], writes=[bdsc])
                em("vector", (lambda E: E.tensor_scalar_add(dsc[:, 0:16], dsc[:, 0:16], COS_EPS)),
                   reads=[bdsc], writes=[bdsc])
                em("vector", (lambda E: E.reciprocal(dsc[:, 32:48], dsc[:, 0:16])),
                   reads=[bdsc], writes=[bdsc])
                em("vector", (lambda E: E.tensor_tensor(dsc[:, 48:64], cosqk[:, 0:16],
                                                        dsc[:, 32:48], ALU.mult)),
                   reads=[bdsc, bcos], writes=[bdsc])
                em("vector", (lambda E: E.tensor_tensor(dsc[:, 64:80], sinqk[:, 0:16],
                                                        dsc[:, 32:48], ALU.mult)),
                   reads=[bdsc, bsin], writes=[bdsc])
                for hh in range(2):
                    cu_c = PB[3 + hh]
                    cu_s = PB[5 + hh]
                    q0 = hh * 8
                    dst = csin[:, hh * 512:(hh + 1) * 512]
                    tmp = csin[:, 1024 + hh * 512:1024 + (hh + 1) * 512]
                    em("vector", (lambda E, dst=dst, cu_c=cu_c, q0=q0: E.tensor_tensor(
                        dst.rearrange("p (i h) -> p i h", i=8),
                        cu_c[:, :].rearrange("p (i h) -> p i h", i=8),
                        dsc[:, 48 + q0:48 + q0 + 8].broadcast_to([128, 8, 64]), ALU.mult)),
                       reads=[bpb[3 + hh], bdsc, bcsin], writes=[bcsin])
                    em("vector", (lambda E, tmp=tmp, cu_s=cu_s, q0=q0: E.tensor_tensor(
                        tmp.rearrange("p (i h) -> p i h", i=8),
                        cu_s[:, :].rearrange("p (i h) -> p i h", i=8),
                        dsc[:, 64 + q0:64 + q0 + 8].broadcast_to([128, 8, 64]), ALU.mult)),
                       reads=[bpb[5 + hh], bdsc, bcsin], writes=[bcsin])
                    em("vector", (lambda E, dst=dst, tmp=tmp: E.tensor_add(dst, dst, tmp)),
                       reads=[bcsin], writes=[bcsin])
                for g in range(2):
                    pb = PB[3 + g]
                    fns = []
                    for c4 in range(4):
                        c = g * 4 + c4
                        fns.append(lambda E, c=c, c4=c4, pb=pb: E.transpose(
                            pb[:, c4 * 128:(c4 + 1) * 128],
                            csin[:, c * 128:(c + 1) * 128], sI[:, :]))
                    em("tensor", fns, reads=[bcsin, bconsts], writes=[bpb[3 + g]])
                    em("scalar", (lambda E, pb=pb, g=g: E.copy(
                        out=hT_hi[:, g * 512:(g + 1) * 512], in_=pb[:, :])),
                       reads=[bpb[3 + g]], writes=[bcsin])
                    em("vector", (lambda E, pb=pb, g=g: E.tensor_tensor(
                        hT_lo[:, g * 512:(g + 1) * 512], pb[:, :],
                        hT_hi[:, g * 512:(g + 1) * 512], ALU.subtract)),
                       reads=[bpb[3 + g], bcsin], writes=[bcsin])
                for hh in range(2):
                    fns = []
                    for k in range(KC):
                        lh = hT_hi[:, k * 128:(k + 1) * 128]
                        ll = hT_lo[:, k * 128:(k + 1) * 128]
                        fns += mm3_group(PB[5 + hh][:, :], lh, ll,
                                         wo_hi[:, k * 1024 + hh * 512:k * 1024 + (hh + 1) * 512],
                                         wo_lo[:, k * 1024 + hh * 512:k * 1024 + (hh + 1) * 512],
                                         first=(k == 0), last=(k == KC - 1))
                    em("tensor", fns, reads=[bcsin, b_wo], writes=[bpb[5 + hh]])
            def back2b(tt):
                t = t0 + tt
                xt = XRES[:, t * 1024:(t + 1) * 1024]
                em("vector", (lambda E, xt=xt, t=t: E.tensor_scalar(
                    scr[:, :], xt, murs[:, 2 * t:2 * t + 1], murs[:, 2 * t + 1:2 * t + 2],
                    ALU.subtract, ALU.mult)),
                   reads=[bx[t], bmurs], writes=[bscr])
                em("vector", (lambda E: E.tensor_add(scr[:, 0:512], scr[:, 0:512], PB[5][:, :])),
                   reads=[bpb[5]], writes=[bscr])
                em("vector", (lambda E: E.tensor_add(scr[:, 512:1024], scr[:, 512:1024], PB[6][:, :])),
                   reads=[bpb[6]], writes=[bscr])
                em("vector", (lambda E, xt=xt: E.tensor_add(xt, xt, scr[:, :])),
                   reads=[bscr], writes=[bx[t]])
                ln_stats(scr[:, :], [bscr], 4, 5, blnst,
                         sq=csin[:, 1024:2048], sq_buf=bcsin)
                normalize(scr[:, :], bscr, scr[:, :], [bscr], 4, 5, blnst)
                em("vector", (lambda E: E.tensor_mul(scr[:, :], scr[:, :], sln2w)),
                   reads=[bsln], writes=[bscr])
                em("vector", (lambda E: E.tensor_add(scr[:, :], scr[:, :], sln2b)),
                   reads=[bsln], writes=[bscr])
                transpose_split(
                    scr[:, :], bscr,
                    dst_hi_mk=lambda g, tt=tt: arnb_y2_hi_r[:, g * 4:(g + 1) * 4, tt * 128:(tt + 1) * 128],
                    dst_lo_mk=lambda g, tt=tt: arnb_y2_lo_r[:, g * 4:(g + 1) * 4, tt * 128:(tt + 1) * 128],
                    dst_bufs=[b_y2T], banks=(3, 4))

            if do_att and att_stop >= 1:
                front(0)
                back1(0)
                for tt in range(tiles_per_seg):
                    back2(tt)
                    back2b(tt)
                    if tt + 1 < tiles_per_seg:
                        front(tt + 1)
                        back1(tt + 1)

            q_tok = qtiles * 128
            assert 64 * q_tok <= 16384
            h1_hi = ARNB[:, 0:32 * q_tok]
            h1_lo = ARNB[:, 32 * q_tok:64 * q_tok]
            b_h1 = brole[2]
            toks_w = arena_turnover([b_wv, b_wo, b_wqk])
            w1stg = stg
            b_w1stg = barnc[4]
            stg2 = arnc_f32[:, 4096:4608]
            b_stg2 = barnc[7]
            stgs = (w1stg, stg2)
            b_stgs = (b_w1stg, b_stg2)

            def w1_hi(sl):
                return ARNC[:, sl * 2048: sl * 2048 + 1024]

            def w1_lo(sl):
                return ARNC[:, sl * 2048 + 1024: sl * 2048 + 2048]


            b_w1 = [barnc[0], barnc[1]]

            def w2_hi(sl):
                return ARNC[:, 4096 + sl * 2048: 4096 + sl * 2048 + 1024]

            def w2_lo(sl):
                return ARNC[:, 4096 + sl * 2048 + 1024: 4096 + sl * 2048 + 2048]

            b_w2 = [barnc[2], barnc[6]]
            w2stg = stg
            b_w2stg = barnc[4]
            em("sync", (lambda E: E.dma_start(out=sb2bc, in_=b2bc_h[:, :])),
               writes=[bsln], sem=dBC, inc=16)
            for b in (b_w1[0], b_w1[1], b_w2[0], b_w2[1], b_h1, b_w1stg, b_w2stg):
                b.r.extend(toks_w)

            nquart = tiles_per_seg // qtiles
            for q in (range(nquart) if (do_mlp and do_att) else []):
                qt0 = t0 + q * qtiles
                seg_off = q * qtiles * 128
                def load_w1(o, sl):
                    for wv_w, q_eng, q_sem in ((0, "sync", dW1), (1, "scalar", dW1s)):
                        em(q_eng, (lambda E, o=o, wv_w=wv_w: E.dma_start(
                            out=stgs[wv_w], in_=w1_h[o, :, wv_w * 512:(wv_w + 1) * 512])),
                           writes=[b_stgs[wv_w]], sem=q_sem, inc=16)
                        em("vector", (lambda E, sl=sl, wv_w=wv_w: E.tensor_copy(
                            w1_hi(sl)[:, wv_w * 512:(wv_w + 1) * 512], stgs[wv_w])),
                           reads=[b_stgs[wv_w]], writes=[b_w1[sl]])
                        em("vector", (lambda E, sl=sl, wv_w=wv_w: E.tensor_tensor(
                            w1_lo(sl)[:, wv_w * 512:(wv_w + 1) * 512], stgs[wv_w],
                            w1_hi(sl)[:, wv_w * 512:(wv_w + 1) * 512], ALU.subtract)),
                           reads=[b_stgs[wv_w], b_w1[sl]], writes=[b_w1[sl], b_stgs[wv_w]])

                load_w1(0, 0)
                for o in range(32):
                    sl = o % 2
                    if o + 1 < 32:
                        load_w1(o + 1, 1 - sl)
                    pb = PB[o % 4]
                    pbuf = bpb[o % 4]
                    fns = []
                    for k in range(KC):
                        rh = y2T_hi(k)[:, seg_off:seg_off + q_tok]
                        rl = y2T_lo(k)[:, seg_off:seg_off + q_tok]
                        fns += mm3_group(pb[:, 0:q_tok],
                                         w1_hi(sl)[:, k * 128:(k + 1) * 128],
                                         w1_lo(sl)[:, k * 128:(k + 1) * 128],
                                         rh, rl, first=(k == 0), last=(k == KC - 1))
                    em("tensor", fns, reads=[b_w1[sl], b_y2T], writes=[pbuf])
                    em("scalar", (lambda E, pb=pb, o=o: E.activation(
                        out=scrF[:, 0:q_tok], in_=pb[:, 0:q_tok], func=AF.Gelu_apprx_tanh,
                        bias=sb1c[:, o:o + 1])),
                       reads=[pbuf, bsb], writes=[bscrF])
                    em("scalar", (lambda E, o=o: E.copy(
                        out=h1_hi[:, o * q_tok:(o + 1) * q_tok], in_=scrF[:, 0:q_tok])),
                       reads=[bscrF], writes=[b_h1])
                    em("vector", (lambda E, o=o: E.tensor_tensor(
                        h1_lo[:, o * q_tok:(o + 1) * q_tok], scrF[:, 0:q_tok],
                        h1_hi[:, o * q_tok:(o + 1) * q_tok], ALU.subtract)),
                       reads=[bscrF, b_h1], writes=[b_h1])
                def load_w2(k, sl):
                    for wv_w, q_eng, q_sem in ((0, "sync", dW2), (1, "scalar", dW2s)):
                        em(q_eng, (lambda E, k=k, wv_w=wv_w: E.dma_start(
                            out=stgs[wv_w], in_=w2_h[k * 128:(k + 1) * 128, wv_w * 512:(wv_w + 1) * 512])),
                           writes=[b_stgs[wv_w]], sem=q_sem, inc=16)
                        em("vector", (lambda E, sl=sl, wv_w=wv_w: E.tensor_copy(
                            w2_hi(sl)[:, wv_w * 512:(wv_w + 1) * 512], stgs[wv_w])),
                           reads=[b_stgs[wv_w]], writes=[b_w2[sl]])
                        em("vector", (lambda E, sl=sl, wv_w=wv_w: E.tensor_tensor(
                            w2_lo(sl)[:, wv_w * 512:(wv_w + 1) * 512], stgs[wv_w],
                            w2_hi(sl)[:, wv_w * 512:(wv_w + 1) * 512], ALU.subtract)),
                           reads=[b_stgs[wv_w], b_w2[sl]], writes=[b_w2[sl], b_stgs[wv_w]])

                load_w2(0, 0)
                for k in range(32):
                    sl = k % 2
                    if k + 1 < 32:
                        load_w2(k + 1, 1 - sl)
                    for ti in range(qtiles):
                        lh = h1_hi[:, k * q_tok + ti * 128: k * q_tok + (ti + 1) * 128]
                        ll = h1_lo[:, k * q_tok + ti * 128: k * q_tok + (ti + 1) * 128]
                        for hh in range(2):
                            pb = PB[ti * 2 + hh]
                            fns = mm3_group(pb[:, :], lh, ll,
                                            w2_hi(sl)[:, hh * 512:(hh + 1) * 512],
                                            w2_lo(sl)[:, hh * 512:(hh + 1) * 512],
                                            first=(k == 0), last=(k == 31))
                            em("tensor", fns, reads=[b_h1, b_w2[sl]], writes=[bpb[ti * 2 + hh]])
                for ti in range(qtiles):
                    t = qt0 + ti
                    xt = XRES[:, t * 1024:(t + 1) * 1024]
                    for hh in range(2):
                        em("vector", (lambda E, xt=xt, ti=ti, hh=hh: E.tensor_add(
                            xt[:, hh * 512:(hh + 1) * 512], xt[:, hh * 512:(hh + 1) * 512],
                            PB[ti * 2 + hh][:, :])),
                           reads=[bpb[ti * 2 + hh]], writes=[bx[t]])
                    em("vector", (lambda E, xt=xt: E.tensor_add(xt, xt, sb2bc)),
                       reads=[bsln], writes=[bx[t]])

            toks = arena_turnover([b_wv, b_wo, b_wqk, b_y2T, b_h1, b_stg,
                                   b_w1[0], b_w1[1], b_w2[0], b_w2[1], b_w1stg, b_w2stg])
            for b in barnb + barnc + brole:
                b.r.extend([tk for tk in toks if tk])

        rowbufs = [bcarry] + list(bbrow) + list(bbrowh) + list(bbrowl)
        if "f1" in phases:
            ffn_phase(0)
            _bp2_handoff()
        toks = arena_turnover(barnb + barnc + brole + rowbufs)
        for lst in (barnb, barnc, brole, rowbufs):
            for b in lst:
                b.r.extend(toks)
        if "att" in phases or "mlp" in phases:
            for seg in range(nseg):
                attn_mlp_segment(seg)
        toks = arena_turnover(barnb + barnc + brole + rowbufs)
        for lst in (barnb, barnc, brole, rowbufs):
            for b in lst:
                b.r.extend(toks)
        if "f2" in phases:
            ffn_phase(1)
            _bp2_handoff()

        store_toks = []
        for t in range(NT):
            tok = em("sync", (lambda E, t=t: E.dma_start(
                out=y_h[t * 128:(t + 1) * 128, :], in_=XRES[:, t * 1024:(t + 1) * 1024])),
                reads=[bx[t]], sem=dST, inc=16)
            store_toks.append(tok)
        em("gpsimd", (lambda E: E.engine_nop() if hasattr(E, "engine_nop") else E.memset(tq[:, 0:1], 0.0)),
           extra=[store_toks[-1]])

        with nc.Block() as block:
            @block.sync
            def _(e):
                for cmd in S.q["sync"]:
                    cmd(e)

            @block.gpsimd
            def _(e):
                for cmd in S.q["gpsimd"]:
                    cmd(e)

            @block.tensor
            def _(e):
                for cmd in S.q["tensor"]:
                    cmd(e)

            @block.scalar
            def _(e):
                for cmd in S.q["scalar"]:
                    cmd(e)

            @block.vector
            def _(e):
                for cmd in S.q["vector"]:
                    cmd(e)

    return nc


def _prep(inputs):
    f32 = np.float32
    import ml_dtypes
    bf16 = ml_dtypes.bfloat16
    x = np.asarray(inputs["L0_attn_x"], f32)
    B, T, _ = x.shape
    WQ = np.asarray(inputs["L0_attn_W_Q"], f32)
    WK = np.asarray(inputs["L0_attn_W_K"], f32)
    WV = np.asarray(inputs["L0_attn_W_V"], f32)
    WO = np.asarray(inputs["L0_attn_W_O"], f32)
    W1 = np.asarray(inputs["L0_attn_W1"], f32)
    b1 = np.asarray(inputs["L0_attn_b1"], f32)
    W2 = np.asarray(inputs["L0_attn_W2"], f32)
    b2 = np.asarray(inputs["L0_attn_b2"], f32)
    gw = np.asarray(inputs["L0_ffn_gate_w"], f32)
    gb = np.asarray(inputs["L0_ffn_gate_b"], f32)
    ew = np.asarray(inputs["L0_ffn_expert_w"], f32)
    eb = np.asarray(inputs["L0_ffn_expert_b"], f32)
    ln1w = np.asarray(inputs["L0_attn_ln1_w"], f32)
    ln1b = np.asarray(inputs["L0_attn_ln1_b"], f32)
    ln2w = np.asarray(inputs["L0_attn_ln2_w"], f32)
    ln2b = np.asarray(inputs["L0_attn_ln2_b"], f32)

    C = np.ascontiguousarray
    com = {}
    com["wv"] = C(np.transpose(WV, (1, 0, 2)).reshape(D, NH * DH))
    com["wqk"] = C(np.concatenate([WQ[:, :, 0].T, WK[:, :, 0].T], axis=1))
    com["wo"] = C(np.transpose(WO, (2, 1, 0)).reshape(NH * DH, D))
    com["w1"] = C(W1.reshape(8, 128, 32, 128).transpose(2, 1, 0, 3).reshape(32, 128, 1024))
    com["w2"] = C(W2)
    com["b1c"] = C(b1.reshape(32, 128).T)
    com["b2bc"] = C(np.broadcast_to(b2[None, :], (128, D)))
    expT = np.transpose(ew, (0, 2, 1))
    com["expw"] = C(expT.reshape(NE, 8, 128, 2, 512).transpose(0, 3, 2, 1, 4).reshape(NE, 2, 128, 4096))
    com["expb"] = C(eb)
    com["gw"] = C(gw.T.reshape(8, 128, 8).transpose(1, 0, 2).reshape(128, 64))
    com["gb"] = C(gb.reshape(1, 8))
    com["ln1w"] = C(np.broadcast_to(ln1w[None, :], (128, D)))
    com["ln1b"] = C(np.broadcast_to(ln1b[None, :], (128, D)))
    com["ln2w"] = C(np.broadcast_to(ln2w[None, :], (128, D)))
    com["ln2b"] = C(np.broadcast_to(ln2b[None, :], (128, D)))
    total = float(B * T * 2)
    CONST_VALS = [0.0, 1.0, PI / 2, PI / 4, LN_EPS, COS_EPS, 1.0 / D, 0.5 / total]
    com["consts"] = C(np.tile(np.asarray(CONST_VALS, f32)[None, :], (128, 1)))
    com["ones_col"] = np.ones((128, 1), f32)
    com["ones_col_b"] = np.ones((128, 1), bf16)
    com["ones_row"] = np.ones((1, 128), f32)
    com["ones2"] = np.ones((2, 128), bf16)
    com["u128"] = C(np.triu(np.ones((128, 128), f32)))
    com["ident"] = np.eye(128, dtype=f32)
    shards = [C(x[i]) for i in range(B)]
    return com, shards, B, T


def kernel(**inputs):
    com, shards, B, T = _prep(inputs)
    nc = build_nc(T, n_cores=B)
    in_maps = [dict(com, x=s) for s in shards]
    res = run_bass_kernel_spmd(nc, in_maps, core_ids=list(range(B)))
    out = np.stack([np.asarray(res.results[i]["y"]) for i in range(B)], axis=0)
    return out.astype(np.float32)


# version 19
# speedup vs baseline: 1.0146x; 1.0146x over previous
import contextlib
import numpy as np
import concourse.bass as bass
import concourse.mybir as mybir
from concourse.bass_utils import run_bass_kernel_spmd

F32 = mybir.dt.float32
BF16 = mybir.dt.bfloat16
AF = mybir.ActivationFunctionType
ALU = mybir.AluOpType
AX = mybir.AxisListType

D = 1024
KC = 8
NH = 16
DH = 64
DMLP = 4096
NE = 8
LN_EPS = 1e-5
COS_EPS = 1e-6
PI = float(np.pi)

ENGINES = ("tensor", "vector", "scalar", "sync", "gpsimd")


class Buf:
    __slots__ = ("w", "r")

    def __init__(self):
        self.w = None
        self.r = []


class Sched:
    def __init__(self):
        self.q = {e: [] for e in ENGINES}
        self.cnt = {}
        self.waited = {e: {} for e in ENGINES}
        self.esem = {}

    def _merge(self, eng, toks):
        m = {}
        for (s, v) in toks:
            if v > m.get(s, -1):
                m[s] = v
        need = []
        for s, v in m.items():
            if self.waited[eng].get(s, -1) < v:
                need.append((s, v))
                self.waited[eng][s] = v
        return need

    def emit(self, eng, fns, reads=(), writes=(), extra=(), sem=None, inc=1):
        if not isinstance(fns, (list, tuple)):
            fns = [fns]
        toks = list(extra)
        for b in reads:
            if b.w:
                toks.append(b.w)
        for b in writes:
            if b.w:
                toks.append(b.w)
            toks.extend(b.r)
        need = self._merge(eng, toks)
        if sem is None:
            sem = self.esem[eng]
        self.cnt[sem] = self.cnt.get(sem, 0) + inc
        tok = (sem, self.cnt[sem])

        def cmd(E, fns=tuple(fns), need=tuple(need), sem=sem, inc=inc):
            for s, v in need:
                E.wait_ge(s, v)
            ins = None
            for f in fns:
                ins = f(E)
            ins.then_inc(sem, inc)

        self.q[eng].append(cmd)
        for b in reads:
            b.r.append(tok)
        for b in writes:
            b.w = tok
            b.r = []
        return tok

    def barrier_tokens(self):
        return [(s, v) for s, v in self.cnt.items()]


def build_nc(T, n_cores=8, phases=("f1", "att", "mlp", "f2"), ffn_parts=("a", "cc", "b"), att_stop=99, no_affine=False):
    NT = T // 128
    assert T % 256 == 0
    nseg = 2 if NT >= 8 else 1
    tiles_per_seg = NT // nseg
    qtiles = 2 if tiles_per_seg % 2 == 0 else 1
    nc = bass.Bass("TRN2", target_bir_lowering=False, debug=False)

    dram = {}

    def din(name, shape, dt=F32):
        dram[name] = nc.dram_tensor(name, list(shape), dt, kind="ExternalInput").ap()
        return dram[name]

    x_h = din("x", (T, D))
    wv_h = din("wv", (D, NH * DH))
    wqk_h = din("wqk", (D, 2 * NH))
    wo_h = din("wo", (NH * DH, D))
    w1_h = din("w1", (32, 128, 1024))
    w2_h = din("w2", (DMLP, D))
    b1c_h = din("b1c", (128, 32))
    b2bc_h = din("b2bc", (128, D))
    expw_h = din("expw", (NE, 2, 128, 4096))
    expb_h = din("expb", (NE, D))
    gw_h = din("gw", (128, 8 * 8))
    gb_h = din("gb", (1, 8))
    ln1w_h = din("ln1w", (128, D))
    ln1b_h = din("ln1b", (128, D))
    ln2w_h = din("ln2w", (128, D))
    ln2b_h = din("ln2b", (128, D))
    consts_h = din("consts", (128, 8))
    onesc_h = din("ones_col", (128, 1))
    onescb_h = din("ones_col_b", (128, 1), BF16)
    ones2_h = din("ones2", (2, 128), BF16)
    u128_h = din("u128", (128, 128))
    ident_h = din("ident", (128, 128))

    y_h = nc.dram_tensor("y", [T, D], F32, kind="ExternalOutput").ap()

    cc_bufs = []
    for i in range(2):
        cc_bufs.append((nc.dram_tensor(f"cc_in{i}", [1, 8], F32),
                        nc.dram_tensor(f"cc_out{i}", [1, 8], F32)))

    S = Sched()

    with contextlib.ExitStack() as ctx:
        E = ctx.enter_context

        XRES = E(nc.sbuf_tensor("XRES", [128, NT * 1024], F32))
        arnb_size = max(16 * T, 16384 + 16 * (T // nseg))
        ARNB = E(nc.sbuf_tensor("ARNB", [128, arnb_size], BF16))
        ARNC = E(nc.sbuf_tensor("ARNC", [128, 17920], BF16))
        scr = E(nc.sbuf_tensor("scr", [128, 1024], F32))
        scr2 = E(nc.sbuf_tensor("scr2", [128, 1024], F32))
        _scr2_b16 = scr2[:, :].bitcast(BF16)
        bp2 = (_scr2_b16[0:2, 0:512], _scr2_b16[0:2, 512:1024])
        scrF = E(nc.sbuf_tensor("scrF", [128, 256], F32))
        csin = E(nc.sbuf_tensor("csin", [128, 2080], F32))
        ROWS = E(nc.sbuf_tensor("ROWS", [1, 2080], F32))
        carry = ROWS[0:1, 0:2080]
        _rows_b16 = ROWS[0:1, :].bitcast(BF16)
        tq = E(nc.sbuf_tensor("tq", [128, 32], F32))
        cosqk = E(nc.sbuf_tensor("cosqk", [128, 32], F32))
        sinqk = E(nc.sbuf_tensor("sinqk", [128, 32], F32))
        dsc = E(nc.sbuf_tensor("dsc", [128, 112], F32))
        murs = E(nc.sbuf_tensor("murs", [128, 2 * NT], F32))
        lnst = E(nc.sbuf_tensor("lnst", [128, 8], F32))
        gate = E(nc.sbuf_tensor("gate", [128, 8], F32))
        top8 = E(nc.sbuf_tensor("top8", [128, 8], F32))
        vind = E(nc.sbuf_tensor("vind", [128, 8], BF16))
        cntb = E(nc.sbuf_tensor("cntb", [128, 8], F32))
        cntr = E(nc.sbuf_tensor("cntr", [1, 8], F32))
        brow = (ROWS[0:1, 0:512], ROWS[0:1, 1024:1536])
        browh = (_rows_b16[0:1, 1024:1536], _rows_b16[0:1, 3072:3584])
        browl = (_rows_b16[0:1, 1536:2048], _rows_b16[0:1, 3584:4096])
        gbh = E(nc.sbuf_tensor("gbh", [1, 8], BF16))
        gbl = E(nc.sbuf_tensor("gbl", [1, 8], BF16))
        gbr = E(nc.sbuf_tensor("gbr", [1, 8], F32))
        sconst = E(nc.sbuf_tensor("sconst", [128, 8], F32))
        sonesc = E(nc.sbuf_tensor("sonesc", [128, 1], F32))
        sonescb = E(nc.sbuf_tensor("sonescb", [128, 1], BF16))
        sones2 = E(nc.sbuf_tensor("sones2", [2, 128], BF16))
        sU = E(nc.sbuf_tensor("sU", [128, 128], F32))
        sI = E(nc.sbuf_tensor("sI", [128, 128], F32))
        gwst = E(nc.sbuf_tensor("gwst", [128, 64], F32))
        gwhi = E(nc.sbuf_tensor("gwhi", [128, 64], BF16))
        gwlo = E(nc.sbuf_tensor("gwlo", [128, 64], BF16))
        BC = E(nc.sbuf_tensor("BC", [128, 4 * D], F32))
        sln1w = BC[:, 0:D]
        sln1b = BC[:, D:2 * D]
        sln2w = BC[:, 2 * D:3 * D]
        sln2b = BC[:, 3 * D:4 * D]
        sb2bc = BC[:, 0:D]
        sb1c = E(nc.sbuf_tensor("sb1c", [128, 32], F32))

        cnt_scale = 0.5 / float(n_cores * T * 2)
        CONST_VALS = [0.0, 1.0, PI / 2, PI / 4, LN_EPS, COS_EPS, 1.0 / D, cnt_scale]
        for j, v in enumerate(CONST_VALS):
            nc.const_aps.register(F32, float(v), sconst[:, j:j + 1])

        PB = [E(nc.psum_tensor(f"PB{i}", [128, 512], F32)) for i in range(8)]

        stn = E(nc.semaphore("stn"))
        vec = E(nc.semaphore("vec"))
        scl = E(nc.semaphore("scl"))
        gps = E(nc.semaphore("gps"))
        dIN = E(nc.semaphore("dIN"))
        dX = [E(nc.semaphore(f"dX{i}")) for i in range(4)]
        dE0 = E(nc.semaphore("dE0"))
        dE1 = E(nc.semaphore("dE1"))
        dB = (E(nc.semaphore("dB0")), E(nc.semaphore("dB1")))
        dBc = (E(nc.semaphore("dBc0")), E(nc.semaphore("dBc1")))
        dWV = E(nc.semaphore("dWV"))
        dWVs = E(nc.semaphore("dWVs"))
        dWOs = E(nc.semaphore("dWOs"))
        dWO = E(nc.semaphore("dWO"))
        dWQ = E(nc.semaphore("dWQ"))
        dW1 = E(nc.semaphore("dW1"))
        dW2 = E(nc.semaphore("dW2"))
        dST = E(nc.semaphore("dST"))
        dCC = E(nc.semaphore("dCC"))
        dC0 = E(nc.semaphore("dC0"))
        dC1 = E(nc.semaphore("dC1"))
        dWVc = E(nc.semaphore("dWVc"))
        dWOc = E(nc.semaphore("dWOc"))
        dWQc = E(nc.semaphore("dWQc"))
        dW1c = (E(nc.semaphore("dW1c0")), E(nc.semaphore("dW1c1")))
        dW2c = (E(nc.semaphore("dW2c0")), E(nc.semaphore("dW2c1")))
        dW1s = E(nc.semaphore("dW1s"))
        dW2s = E(nc.semaphore("dW2s"))
        dBC = E(nc.semaphore("dBC"))
        dCR = E(nc.semaphore("dCR"))
        S.esem.update(tensor=stn, vector=vec, scalar=scl, gpsimd=gps, sync=None)

        bx = [Buf() for _ in range(NT)]
        bscr, bscr2, bscrF = Buf(), Buf(), Buf()
        bbp2 = (Buf(), Buf())
        bcsin, bcarry = Buf(), Buf()
        btq, bcos, bsin, bdsc = Buf(), Buf(), Buf(), Buf()
        bmurs = Buf()
        blnst = Buf()
        bgate, btop8, bvind, bcntb, bcntr = Buf(), Buf(), Buf(), Buf(), Buf()
        bbrow = (Buf(), Buf())
        bbrowh = (Buf(), Buf())
        bbrowl = (Buf(), Buf())
        bgb = Buf()
        bgw = Buf()
        bconsts = Buf()
        bpb = [Buf() for _ in range(8)]
        barnb = [Buf() for _ in range(NT)]
        barnc = [Buf() for _ in range(8)]
        brole = [Buf() for _ in range(4)]
        bsln = Buf()
        bsb = Buf()

        arnb_f32 = ARNB[:, :].bitcast(F32)
        arnc_f32 = ARNC[:, :].bitcast(F32)

        def arena_turnover(bufs):
            toks = []
            for b in bufs:
                if b.w:
                    toks.append(b.w)
                toks.extend(b.r)
                b.w = None
                b.r = []
            return toks

        em = S.emit

        ng = max(1, NT // 4)
        for t in range(NT):
            em("sync", (lambda E, t=t: E.dma_start(
                out=XRES[:, t * 1024:(t + 1) * 1024], in_=x_h[t * 128:(t + 1) * 128, :])),
                writes=[bx[t]], sem=dX[t // ng], inc=16)
        for g in range(4):
            lo, hi = g * ng, min((g + 1) * ng, NT)
            if lo >= NT:
                break
            tok = (dX[g], S.cnt[dX[g]])
            for t in range(lo, hi):
                bx[t].w = tok
        for hb, sb, b in ((consts_h, sconst, bconsts), (onesc_h, sonesc, bconsts),
                          (onescb_h, sonescb, bconsts),
                          (ones2_h, sones2, bconsts),
                          (u128_h, sU, bconsts), (ident_h, sI, bconsts),
                          (gw_h, gwst, bgw), (gb_h, gbr, bgb),
                          (b1c_h, sb1c, bsb)):
            em("sync", (lambda E, hb=hb, sb=sb: E.dma_start(out=sb[:, :], in_=hb[:, :])),
               writes=[b], sem=dIN, inc=16)
        _init_total = (dIN, S.cnt[dIN])
        for b in [bconsts, bgw, bgb, bsln, bsb]:
            b.w = _init_total

        em("scalar", (lambda E: E.copy(out=gwhi[:, :], in_=gwst[:, :])), reads=[bgw], writes=[bgw])
        em("vector", (lambda E: E.tensor_sub(gwlo[:, :], gwst[:, :], gwhi[:, :])),
           reads=[bgw], writes=[bgw])
        em("scalar", (lambda E: E.copy(out=gbh[:, :], in_=gbr[:, :])), reads=[bgb], writes=[bgb])
        em("vector", (lambda E: E.tensor_sub(gbl[:, :], gbr[:, :], gbh[:, :])),
           reads=[bgb], writes=[bgb])

        def ln_stats(src_ap, src_bufs, mu_slot, r_slot, dst_buf, keep_vpe=None,
                     sq=None, sq_buf=None):
            s1 = lnst[:, mu_slot:mu_slot + 1]
            s2 = lnst[:, r_slot:r_slot + 1]
            t3 = lnst[:, 6:7]
            t4 = lnst[:, 7:8]
            em("vector", (lambda E: E.reduce_sum(s1, src_ap, axis=AX.X)),
               reads=src_bufs, writes=[dst_buf])
            if sq is None:
                sq, sq_buf = scr2[:, :], bscr2
            em("scalar", (lambda E, sq=sq: E.activation(out=sq, in_=src_ap, func=AF.Square)),
               reads=src_bufs, writes=[sq_buf])
            em("vector", (lambda E, sq=sq: E.reduce_sum(s2, sq, axis=AX.X)),
               reads=[sq_buf], writes=[dst_buf])
            em("vector", (lambda E: E.tensor_scalar_mul(s1, s1, 1.0 / D)),
               reads=[dst_buf], writes=[dst_buf])
            em("vector", (lambda E: E.tensor_scalar_mul(s2, s2, 1.0 / D)),
               reads=[dst_buf], writes=[dst_buf])
            em("vector", (lambda E: E.tensor_mul(t3, s1, s1)), reads=[dst_buf], writes=[dst_buf])
            em("vector", (lambda E: E.tensor_sub(s2, s2, t3)), reads=[dst_buf], writes=[dst_buf])
            em("vector", (lambda E: E.tensor_scalar_add(s2, s2, LN_EPS)),
               reads=[dst_buf], writes=[dst_buf])
            if keep_vpe is not None:
                em("vector", (lambda E: E.tensor_copy(lnst[:, keep_vpe:keep_vpe + 1], s2)),
                   reads=[dst_buf], writes=[dst_buf])
            em("scalar", (lambda E: E.activation(out=t3, in_=s2, func=AF.Sqrt)),
               reads=[dst_buf], writes=[dst_buf])
            em("vector", (lambda E: E.reciprocal(t4, t3)), reads=[dst_buf], writes=[dst_buf])
            em("vector", (lambda E: E.tensor_mul(t3, t4, t4)), reads=[dst_buf], writes=[dst_buf])
            em("vector", (lambda E: E.tensor_mul(t3, t3, s2)), reads=[dst_buf], writes=[dst_buf])
            em("vector", (lambda E: E.tensor_scalar(t3, t3, -0.5, 1.5, ALU.mult, ALU.add)),
               reads=[dst_buf], writes=[dst_buf])
            em("vector", (lambda E: E.tensor_mul(s2, t4, t3)), reads=[dst_buf], writes=[dst_buf])

        def normalize(dst_ap, dst_buf, src_ap, src_bufs, mu_slot, r_slot, stats_buf):
            em("vector", (lambda E: E.tensor_scalar(
                dst_ap, src_ap, lnst[:, mu_slot:mu_slot + 1], lnst[:, r_slot:r_slot + 1],
                ALU.subtract, ALU.mult)),
               reads=list(src_bufs) + [stats_buf], writes=[dst_buf])

        def transpose_split(src, src_buf, dst_hi_mk, dst_lo_mk, dst_bufs,
                            affine=None, banks=(0, 1)):
            for g in range(2):
                bk = banks[g]
                pb = PB[bk]
                fns = []
                for c4 in range(4):
                    c = g * 4 + c4
                    fns.append(lambda E, c=c, c4=c4, pb=pb: E.transpose(
                        pb[:, c4 * 128:(c4 + 1) * 128], src[:, c * 128:(c + 1) * 128], sI[:, :]))
                ttok = em("tensor", fns, reads=[src_buf, bconsts], writes=[bpb[bk]])
                if affine is None:
                    hi_ap = dst_hi_mk(g)
                    lo_ap = dst_lo_mk(g)
                    em("scalar", (lambda E, pb=pb, hi_ap=hi_ap: E.copy(out=hi_ap, in_=pb[:, :])),
                       reads=[bpb[bk]], writes=dst_bufs)
                    em("vector", (lambda E, pb=pb, hi_ap=hi_ap, lo_ap=lo_ap: E.tensor_tensor(
                        lo_ap, pb[:, :].rearrange("p (c n) -> p c n", c=4), hi_ap, ALU.subtract)),
                       reads=[bpb[bk]] + list(dst_bufs), writes=dst_bufs)
                else:
                    wc, bc, abuf = affine
                    for c4 in range(4):
                        c = g * 4 + c4
                        sl = pb[:, c4 * 128:(c4 + 1) * 128]
                        hi_ap = dst_hi_mk(c)
                        lo_ap = dst_lo_mk(c)
                        em("scalar", (lambda E, sl=sl, hi_ap=hi_ap, c=c, wc=wc, bc=bc:
                                      E.activation(out=hi_ap, in_=sl, func=AF.Identity,
                                                   bias=bc[:, c:c + 1], scale=wc[:, c:c + 1])),
                           reads=[bpb[g], abuf], writes=dst_bufs)
                        em("vector", (lambda E, sl=sl, c=c, wc=wc, bc=bc: E.scalar_tensor_tensor(
                            scrF[:, 0:128], sl, wc[:, c:c + 1],
                            bc[:, c:c + 1].broadcast_to([128, 128]),
                            ALU.mult, ALU.add)),
                           reads=[bpb[g], abuf], writes=[bscrF])
                        em("vector", (lambda E, hi_ap=hi_ap, lo_ap=lo_ap: E.tensor_tensor(
                            lo_ap, scrF[:, 0:128], hi_ap, ALU.subtract)),
                           reads=[bscrF] + list(dst_bufs), writes=dst_bufs)

        def mm3_group(pb_ap, lhs_hi, lhs_lo, rhs_hi, rhs_lo, first, last):
            return [
                (lambda E: E.matmul(pb_ap, lhs_hi, rhs_hi, start=first, stop=False)),
                (lambda E: E.matmul(pb_ap, lhs_hi, rhs_lo, start=False, stop=False)),
                (lambda E: E.matmul(pb_ap, lhs_lo, rhs_hi, start=False, stop=last)),
            ]

        def xnT_hi(c):
            return ARNB[:, c * T:(c + 1) * T]

        def xnT_lo(c):
            return ARNB[:, NT * 1024 + c * T: NT * 1024 + (c + 1) * T]

        arnb_r = ARNB[:, 0:16 * T].rearrange("p (c n) -> p c n", c=2 * KC)

        def ffn_phase(call_idx):
            cc_in, cc_out = cc_bufs[call_idx]
            do_a = "a" in ffn_parts
            do_cc = "cc" in ffn_parts
            do_b = "b" in ffn_parts
            pa_buf = (scr, scr2)
            pa_bbuf = (bscr, bscr2)

            def pa_s1(t):
                xt = XRES[:, t * 1024:(t + 1) * 1024]
                b = pa_buf[t % 2]
                bb = pa_bbuf[t % 2]
                ln_stats(xt, [bx[t]], 0, 1, blnst, sq=b[:, :], sq_buf=bb)
                normalize(b[:, :], bb, xt, [bx[t]], 0, 1, blnst)

            def pa_s2(t):
                b = pa_buf[t % 2]
                bb = pa_bbuf[t % 2]
                transpose_split(
                    b[:, :], bb,
                    dst_hi_mk=lambda g, t=t: arnb_r[:, g * 4:(g + 1) * 4, t * 128:(t + 1) * 128],
                    dst_lo_mk=lambda g, t=t: arnb_r[:, KC + g * 4:KC + (g + 1) * 4, t * 128:(t + 1) * 128],
                    dst_bufs=[barnb[t]])
                gpb = PB[2 + (t % 2)]
                gbuf = bpb[2 + (t % 2)]
                fns = []
                for k in range(KC):
                    lh = xnT_hi(k)[:, t * 128:(t + 1) * 128]
                    ll = xnT_lo(k)[:, t * 128:(t + 1) * 128]
                    fns += mm3_group(gpb[:, 0:8], lh, ll,
                                     gwhi[:, k * 8:(k + 1) * 8], gwlo[:, k * 8:(k + 1) * 8],
                                     first=(k == 0), last=False)
                fns.append(lambda E, gpb=gpb: E.matmul(gpb[:, 0:8], sones2[0:1, :], gbh[0:1, :],
                                                       start=False, stop=False))
                fns.append(lambda E, gpb=gpb: E.matmul(gpb[:, 0:8], sones2[0:1, :], gbl[0:1, :],
                                                       start=False, stop=True))
                em("tensor", fns, reads=[barnb[t], bgw, bgb, bconsts], writes=[gbuf])

            def pa_s3(t):
                gpb = PB[2 + (t % 2)]
                gbuf = bpb[2 + (t % 2)]
                em("vector", (lambda E, gpb=gpb: E.tensor_copy(gate[:, :], gpb[:, 0:8])),
                   reads=[gbuf], writes=[bgate])
                em("vector", (lambda E: E.max(top8[:, :], gate[:, :])),
                   reads=[bgate], writes=[btop8])
                em("vector", (lambda E: E.tensor_single_scalar(
                    vind[:, :], gate[:, :], top8[:, 1:2], ALU.is_ge)),
                   reads=[bgate, btop8], writes=[bvind])
                em("tensor", (lambda E, t=t: E.matmul(PB[4][0:1, 0:8], sonescb[:, :], vind[:, :],
                                                      start=(t == 0), stop=(t == NT - 1))),
                   reads=[bvind, bconsts],
                   writes=([bpb[4]] if t in (0, NT - 1) else []))

            if do_a:
                pa_s1(0)
                for t in range(NT):
                    pa_s2(t)
                    if t + 1 < NT:
                        pa_s1(t + 1)
                    pa_s3(t)
            if not (do_a and do_cc and do_b):
                return
            em("scalar", (lambda E: E.copy(out=cntr[:, :], in_=PB[4][0:1, 0:8])),
               reads=[bpb[4]], writes=[bcntr])
            em("gpsimd", (lambda E: E.dma_start(out=cc_in[:, :], in_=cntr[:, :])),
               reads=[bcntr], sem=dCC, inc=16)
            cc_tok = em("gpsimd", (lambda E: E.collective_compute(
                "AllReduce", ALU.add, replica_groups=[list(range(8))],
                ins=[cc_in[:, :]], outs=[cc_out[:, :]])),
               extra=[(dCC, S.cnt.get(dCC, 0))])
            em("gpsimd", (lambda E: E.dma_start(out=cntr[:, :], in_=cc_out[:, :])),
               writes=[bcntr], extra=[cc_tok], sem=dCC, inc=16)
            em("tensor", (lambda E: E.matmul(PB[5][:, 0:8], sU[0:1, :], cntr[0:1, :],
                                             start=True, stop=True)),
               reads=[bcntr, bconsts], writes=[bpb[5]])
            em("scalar", (lambda E: E.activation(out=cntb[:, :], in_=PB[5][:, 0:8],
                                                 func=AF.Identity, scale=cnt_scale)),
               reads=[bpb[5]], writes=[bcntb])

            stage_f = arnc_f32[:, 0:512]
            b_stage = barnc[0]

            def wem_hi(s):
                return ARNC[:, 1024 + s * 8192: 1024 + s * 8192 + 4096]

            def wem_lo(s):
                return ARNC[:, 5120 + s * 8192: 5120 + s * 8192 + 4096]

            b_wem = [barnc[1], barnc[2]]

            dE = (dE0, dE1)

            def load_expert(e, half, s):
                em("gpsimd", (lambda E, e=e, half=half, s=s: E.dma_start(
                    out=wem_hi(s), in_=expw_h[e, half, :, :])),
                   writes=[b_wem[s]], sem=(dC0, dC1)[s], inc=16)
                for wave in range(8):
                    em("sync", (lambda E, e=e, half=half, wave=wave: E.dma_start(
                        out=stage_f[:, :],
                        in_=expw_h[e, half, :, wave * 512:(wave + 1) * 512])),
                       writes=[b_stage], sem=dE[s], inc=16)
                    em("vector", (lambda E, s=s, wave=wave: E.tensor_tensor(
                        wem_lo(s)[:, wave * 512:(wave + 1) * 512],
                        stage_f[:, :],
                        wem_hi(s)[:, wave * 512:(wave + 1) * 512], ALU.subtract)),
                       reads=[b_stage, b_wem[s]], writes=[b_wem[s], b_stage])
                em("sync", (lambda E, e=e, half=half, s=s: E.dma_start(
                    out=brow[s], in_=expb_h[e:e + 1, half * 512:(half + 1) * 512])),
                   writes=[bbrow[s]], sem=dB[s], inc=16)
                em("scalar", (lambda E, s=s: E.copy(out=bp2[s][0:1, :], in_=brow[s])),
                   reads=[bbrow[s]], writes=[bbp2[s]])
                em("vector", (lambda E, s=s: E.tensor_sub(browl[s], brow[s], bp2[s][0:1, :])),
                   reads=[bbrow[s], bbp2[s]], writes=[bbrowl[s]])
                em("gpsimd", (lambda E, s=s: E.dma_start(out=bp2[s][1:2, :], in_=browl[s])),
                   reads=[bbrowl[s]], writes=[bbp2[s]], sem=dBc[s], inc=16)

            def load_expert_cast(e, half, s):
                em("gpsimd", (lambda E, e=e, half=half, s=s: E.dma_start(
                    out=wem_hi(s), in_=expw_h[e, half, :, :])),
                   writes=[b_wem[s]], sem=(dC0, dC1)[s], inc=16)

            def load_expert_wave(e, half, s, wave):
                em("sync", (lambda E, e=e, half=half, wave=wave: E.dma_start(
                    out=stage_f[:, :],
                    in_=expw_h[e, half, :, wave * 512:(wave + 1) * 512])),
                   writes=[b_stage], sem=dE[s], inc=16)
                em("vector", (lambda E, s=s, wave=wave: E.tensor_tensor(
                    wem_lo(s)[:, wave * 512:(wave + 1) * 512],
                    stage_f[:, :],
                    wem_hi(s)[:, wave * 512:(wave + 1) * 512], ALU.subtract)),
                   reads=[b_stage, b_wem[s]], writes=[b_wem[s], b_stage])

            def load_expert_bias(e, half, s):
                em("sync", (lambda E, e=e, half=half, s=s: E.dma_start(
                    out=brow[s], in_=expb_h[e:e + 1, half * 512:(half + 1) * 512])),
                   writes=[bbrow[s]], sem=dB[s], inc=16)
                em("scalar", (lambda E, s=s: E.copy(out=bp2[s][0:1, :], in_=brow[s])),
                   reads=[bbrow[s]], writes=[bbp2[s]])
                em("vector", (lambda E, s=s: E.tensor_sub(browl[s], brow[s], bp2[s][0:1, :])),
                   reads=[bbrow[s], bbp2[s]], writes=[bbrowl[s]])
                em("gpsimd", (lambda E, s=s: E.dma_start(out=bp2[s][1:2, :], in_=browl[s])),
                   reads=[bbrowl[s]], writes=[bbp2[s]], sem=dBc[s], inc=16)

            for b in bbp2:
                if bscr2.w:
                    b.r.append(bscr2.w)
                b.r.extend(bscr2.r)
            seq = [(e, h) for e in range(NE) for h in range(2)]
            load_expert(*seq[0], 0)
            for i, (e, half) in enumerate(seq):
                s = i % 2
                if i + 1 < len(seq):
                    load_expert_cast(*seq[i + 1], 1 - s)
                for t in range(NT):
                    if i + 1 < len(seq):
                        for wv in range(t * 8 // NT, (t + 1) * 8 // NT):
                            load_expert_wave(*seq[i + 1], 1 - s, wv)
                        if t == min(1, NT - 1):
                            load_expert_bias(*seq[i + 1], 1 - s)
                    pb = PB[t % 4]
                    pbuf = bpb[t % 4]
                    fns = []
                    for k in range(KC):
                        lh = xnT_hi(k)[:, t * 128:(t + 1) * 128]
                        ll = xnT_lo(k)[:, t * 128:(t + 1) * 128]
                        fns += mm3_group(pb[:, :], lh, ll,
                                         wem_hi(s)[:, k * 512:(k + 1) * 512],
                                         wem_lo(s)[:, k * 512:(k + 1) * 512],
                                         first=(k == 0), last=False)
                    fns.append(lambda E, pb=pb, s=s: E.matmul(pb[:, :], sones2[:, :], bp2[s][:, :],
                                                              start=False, stop=True))
                    em("tensor", fns,
                       reads=[barnb[t], b_wem[s], bbp2[s], bconsts], writes=[pbuf])
                    half_ap = scr[:, (t % 2) * 512:(t % 2) * 512 + 512]
                    em("scalar", (lambda E, pb=pb, half_ap=half_ap, e=e: E.activation(
                        out=half_ap, in_=pb[:, :], func=AF.Relu, scale=cntb[:, e:e + 1])),
                       reads=[pbuf, bcntb], writes=[bscr])
                    em("vector", (lambda E, t=t, half=half, half_ap=half_ap: E.tensor_add(
                        XRES[:, t * 1024 + half * 512: t * 1024 + half * 512 + 512],
                        XRES[:, t * 1024 + half * 512: t * 1024 + half * 512 + 512],
                        half_ap)),
                       reads=[bscr], writes=[bx[t]])

        seg_T = T // nseg

        def _bp2_handoff():
            toks = arena_turnover(list(bbp2))
            bscr2.r.extend(toks)

        def attn_mlp_segment(seg):
            do_att = "att" in phases
            do_mlp = "mlp" in phases
            t0 = seg * tiles_per_seg
            wo_hi = ARNC[:, 0:8192]
            wo_lo = ARNC[:, 8192:16384]
            stg = arnc_f32[:, 8192:8704]
            wqk_hi = ARNC[:, 17408:17664]
            wqk_lo = ARNC[:, 17664:17920]
            b_wo, b_stg, b_wqk = barnc[3], barnc[4], barnc[5]
            wv_hi = ARNB[:, 0:8192]
            wv_lo = ARNB[:, 8192:16384]
            b_wv = brole[0]
            stgB_att = ARNB[:, 16384:17408].bitcast(F32)
            b_stgB_att = brole[3]

            def y2T_hi(c):
                return ARNB[:, 16384 + c * seg_T: 16384 + (c + 1) * seg_T]

            def y2T_lo(c):
                return ARNB[:, 16384 + KC * seg_T + c * seg_T: 16384 + KC * seg_T + (c + 1) * seg_T]

            arnb_y2_hi_r = ARNB[:, 16384:16384 + KC * seg_T].rearrange("p (c n) -> p c n", c=KC)
            arnb_y2_lo_r = ARNB[:, 16384 + KC * seg_T:16384 + 2 * KC * seg_T].rearrange("p (c n) -> p c n", c=KC)

            b_y2T = brole[1]

            em("gpsimd", (lambda E: E.dma_start(
                out=wv_hi.rearrange("p (c n) -> p c n", c=KC),
                in_=wv_h[:, :].rearrange("(c p) n -> p c n", p=128))),
               writes=[b_wv], sem=dWVc, inc=16)
            for k in range(KC):
                for wv_w, stq, stb, q_eng, q_sem in ((0, stg, b_stg, "sync", dWV),
                                                     (1, None, None, "scalar", dWVs)):
                    if wv_w == 1:
                        stq, stb = stgB_att, b_stgB_att
                    em(q_eng, (lambda E, k=k, wv_w=wv_w, stq=stq: E.dma_start(
                        out=stq, in_=wv_h[k * 128:(k + 1) * 128, wv_w * 512:(wv_w + 1) * 512])),
                       writes=[stb], sem=q_sem, inc=16)
                    em("vector", (lambda E, k=k, wv_w=wv_w, stq=stq: E.tensor_tensor(
                        wv_lo[:, k * 1024 + wv_w * 512:k * 1024 + (wv_w + 1) * 512], stq,
                        wv_hi[:, k * 1024 + wv_w * 512:k * 1024 + (wv_w + 1) * 512], ALU.subtract)),
                       reads=[stb, b_wv], writes=[b_wv, stb])
            em("gpsimd", (lambda E: E.dma_start(
                out=wo_hi.rearrange("p (c n) -> p c n", c=KC),
                in_=wo_h[:, :].rearrange("(c p) n -> p c n", p=128))),
               writes=[b_wo], sem=dWOc, inc=16)
            for k in range(KC):
                for wo_w in range(2):
                    stq, stb = (stg, b_stg) if wo_w == 0 else (stgB_att, b_stgB_att)
                    q_eng, q_sem = ("sync", dWO) if wo_w == 0 else ("scalar", dWOs)
                    em(q_eng, (lambda E, k=k, wo_w=wo_w, stq=stq: E.dma_start(
                        out=stq, in_=wo_h[k * 128:(k + 1) * 128, wo_w * 512:(wo_w + 1) * 512])),
                       writes=[stb], sem=q_sem, inc=16)
                    em("vector", (lambda E, k=k, wo_w=wo_w, stq=stq: E.tensor_tensor(
                        wo_lo[:, k * 1024 + wo_w * 512:k * 1024 + (wo_w + 1) * 512], stq,
                        wo_hi[:, k * 1024 + wo_w * 512:k * 1024 + (wo_w + 1) * 512], ALU.subtract)),
                       reads=[stb, b_wo], writes=[b_wo, stb])
            em("gpsimd", (lambda E: E.dma_start(
                out=wqk_hi.rearrange("p (c n) -> p c n", c=KC),
                in_=wqk_h[:, :].rearrange("(c p) n -> p c n", p=128))),
               writes=[b_wqk], sem=dWQc, inc=16)
            em("sync", (lambda E: E.dma_start(
                out=stg[:, 0:256].rearrange("p (c n) -> p c n", c=KC),
                in_=wqk_h[:, :].rearrange("(c p) n -> p c n", p=128))),
               writes=[b_stg], sem=dWQ, inc=16)
            em("vector", (lambda E: E.tensor_tensor(wqk_lo, stg[:, 0:256], wqk_hi, ALU.subtract)),
               reads=[b_stg, b_wqk], writes=[b_wqk, b_stg])
            for bc_h, bc_dst in ((ln1w_h, sln1w), (ln1b_h, sln1b),
                                 (ln2w_h, sln2w), (ln2b_h, sln2b)):
                em("sync", (lambda E, bc_h=bc_h, bc_dst=bc_dst: E.dma_start(
                    out=bc_dst, in_=bc_h[:, :])), writes=[bsln], sem=dBC, inc=16)
            _stgB_toks = arena_turnover([b_stgB_att])
            b_y2T.r.extend(_stgB_toks)

            scr2_b = scr2[:, :].bitcast(BF16)
            yT_hi = scr2_b[:, 0:1024]
            yT_lo = scr2_b[:, 1024:2048]
            b_yT = bscr2

            hT_hi = csin[:, :].bitcast(BF16)[:, 2048:3072]
            hT_lo = csin[:, :].bitcast(BF16)[:, 3072:4096]

            def front(tt):
                t = t0 + tt
                xt = XRES[:, t * 1024:(t + 1) * 1024]
                ln_stats(xt, [bx[t]], 0, 1, blnst, keep_vpe=2)
                em("vector", (lambda E, t=t: E.tensor_copy(murs[:, 2 * t:2 * t + 2], lnst[:, 0:2])),
                   reads=[blnst], writes=[bmurs])
                normalize(scr[:, :], bscr, xt, [bx[t]], 0, 1, blnst)
                t6 = lnst[:, 6:7]
                t7 = lnst[:, 7:8]
                r1 = lnst[:, 3:4]
                em("vector", (lambda E: E.tensor_mul(t6, lnst[:, 1:2], lnst[:, 1:2])),
                   reads=[blnst], writes=[blnst])
                em("vector", (lambda E: E.tensor_scalar_add(t7, lnst[:, 2:3], -LN_EPS)),
                   reads=[blnst], writes=[blnst])
                em("vector", (lambda E: E.tensor_mul(t6, t6, t7)),
                   reads=[blnst], writes=[blnst])
                em("vector", (lambda E: E.tensor_scalar_add(t6, t6, LN_EPS)),
                   reads=[blnst], writes=[blnst])
                em("scalar", (lambda E: E.activation(out=t7, in_=t6, func=AF.Sqrt)),
                   reads=[blnst], writes=[blnst])
                em("vector", (lambda E: E.reciprocal(r1, t7)),
                   reads=[blnst], writes=[blnst])
                em("vector", (lambda E: E.tensor_mul(t7, r1, r1)),
                   reads=[blnst], writes=[blnst])
                em("vector", (lambda E: E.tensor_mul(t7, t7, t6)),
                   reads=[blnst], writes=[blnst])
                em("vector", (lambda E: E.tensor_scalar(t7, t7, -0.5, 1.5, ALU.mult, ALU.add)),
                   reads=[blnst], writes=[blnst])
                em("vector", (lambda E: E.tensor_mul(r1, r1, t7)),
                   reads=[blnst], writes=[blnst])
                em("vector", (lambda E, r1=r1: E.tensor_scalar_mul(scr[:, :], scr[:, :], r1)),
                   reads=[blnst], writes=[bscr])
                em("vector", (lambda E: E.tensor_mul(scr[:, :], scr[:, :], sln1w)),
                   reads=[bsln], writes=[bscr])
                em("vector", (lambda E: E.tensor_add(scr[:, :], scr[:, :], sln1b)),
                   reads=[bsln], writes=[bscr])
                transpose_split(
                    scr[:, :], bscr,
                    dst_hi_mk=lambda g: yT_hi[:, g * 512:(g + 1) * 512].rearrange("p (c n) -> p c n", c=4),
                    dst_lo_mk=lambda g: yT_lo[:, g * 512:(g + 1) * 512].rearrange("p (c n) -> p c n", c=4),
                    dst_bufs=[b_yT])
                fns0, fns1, fns2 = [], [], []
                for k in range(KC):
                    lh = yT_hi[:, k * 128:(k + 1) * 128]
                    ll = yT_lo[:, k * 128:(k + 1) * 128]
                    fns0 += mm3_group(PB[0][:, :], lh, ll,
                                      wv_hi[:, k * 1024:k * 1024 + 512],
                                      wv_lo[:, k * 1024:k * 1024 + 512],
                                      first=(k == 0), last=(k == KC - 1))
                    fns1 += mm3_group(PB[1][:, :], lh, ll,
                                      wv_hi[:, k * 1024 + 512:(k + 1) * 1024],
                                      wv_lo[:, k * 1024 + 512:(k + 1) * 1024],
                                      first=(k == 0), last=(k == KC - 1))
                    fns2 += mm3_group(PB[7][:, 64:96], lh, ll,
                                      wqk_hi[:, k * 32:(k + 1) * 32],
                                      wqk_lo[:, k * 32:(k + 1) * 32],
                                      first=(k == 0), last=(k == KC - 1))
                em("tensor", fns0, reads=[b_yT, b_wv], writes=[bpb[0]])
                em("tensor", fns1, reads=[b_yT, b_wv], writes=[bpb[1]])
                em("tensor", fns2, reads=[b_yT, b_wqk], writes=[bpb[7]])

            seg_state = {}

            def mid_front(tt):
                seg_state["mid_front"](tt)

            def back1(tt):
                em("scalar", (lambda E: E.activation(out=tq[:, :], in_=PB[7][:, 64:96], func=AF.Tanh)),
                   reads=[bpb[7]], writes=[btq])
                em("scalar", (lambda E: E.activation(out=cosqk[:, :], in_=tq[:, :], func=AF.Sin,
                                                     bias=PI / 2, scale=PI / 4)),
                   reads=[btq], writes=[bcos])
                em("scalar", (lambda E: E.activation(out=sinqk[:, :], in_=tq[:, :], func=AF.Sin,
                                                     bias=0.0, scale=PI / 4)),
                   reads=[btq], writes=[bsin])
                em("vector", (lambda E: E.tensor_tensor(
                    csin[:, 0:512].rearrange("p (i h) -> p i h", i=8),
                    PB[0][:, :].rearrange("p (i h) -> p i h", i=8),
                    cosqk[:, 16:24].broadcast_to([128, 8, 64]), ALU.mult)),
                   reads=[bpb[0], bcos], writes=[bcsin])
                em("vector", (lambda E: E.tensor_tensor(
                    csin[:, 512:1024].rearrange("p (i h) -> p i h", i=8),
                    PB[1][:, :].rearrange("p (i h) -> p i h", i=8),
                    cosqk[:, 24:32].broadcast_to([128, 8, 64]), ALU.mult)),
                   reads=[bpb[1], bcos], writes=[bcsin])
                em("vector", (lambda E: E.tensor_tensor(
                    csin[:, 1024:1536].rearrange("p (i h) -> p i h", i=8),
                    PB[0][:, :].rearrange("p (i h) -> p i h", i=8),
                    sinqk[:, 16:24].broadcast_to([128, 8, 64]), ALU.mult)),
                   reads=[bpb[0], bsin], writes=[bcsin])
                em("vector", (lambda E: E.tensor_tensor(
                    csin[:, 1536:2048].rearrange("p (i h) -> p i h", i=8),
                    PB[1][:, :].rearrange("p (i h) -> p i h", i=8),
                    sinqk[:, 24:32].broadcast_to([128, 8, 64]), ALU.mult)),
                   reads=[bpb[1], bsin], writes=[bcsin])
                em("vector", (lambda E: E.tensor_copy(csin[:, 2048:2064], cosqk[:, 16:32])),
                   reads=[bcos], writes=[bcsin])
                em("vector", (lambda E: E.tensor_copy(csin[:, 2064:2080], sinqk[:, 16:32])),
                   reads=[bsin], writes=[bcsin])

            def back2(tt):
                t = t0 + tt
                xt = XRES[:, t * 1024:(t + 1) * 1024]
                groups = [(0, 512, PB[3], 3), (512, 512, PB[4], 4),
                          (1024, 512, PB[5], 5), (1536, 512, PB[6], 6),
                          (2048, 32, PB[7], 7)]
                for (off, w, pbk, pbi) in groups:
                    fns = []
                    if t > 0:
                        fns.append(lambda E, pbk=pbk, off=off, w=w: E.matmul(
                            pbk[:, 0:w], sU[0:1, :], carry[0:1, off:off + w],
                            start=True, stop=False))
                    fns.append(lambda E, pbk=pbk, off=off, w=w, t=t: E.matmul(
                        pbk[:, 0:w], sU[:, :], csin[:, off:off + w],
                        start=(t == 0), stop=True))
                    em("tensor", fns, reads=[bcsin, bcarry, bconsts], writes=[bpb[pbi]])
                if t < NT - 1:
                    for (off, w, pbk, pbi) in groups:
                        em("scalar", (lambda E, pbk=pbk, off=off, w=w: E.copy(
                            out=csin[96:128, off:off + w], in_=pbk[96:128, 0:w])),
                           reads=[bpb[pbi]], writes=[bcsin])
                    em("sync", (lambda E: E.dma_start(
                        out=carry[0:1, 0:2080], in_=csin[127:128, 0:2080])),
                       reads=[bcsin], writes=[bcarry], sem=dCR, inc=16)
                em("scalar", (lambda E: E.copy(out=dsc[:, 0:32], in_=PB[7][:, 0:32])),
                   reads=[bpb[7]], writes=[bdsc])
                mid_front(tt)
                em("vector", (lambda E: E.tensor_tensor(dsc[:, 32:48], dsc[:, 0:16],
                                                        cosqk[:, 0:16], ALU.mult)),
                   reads=[bcos, bdsc], writes=[bdsc])
                em("vector", (lambda E: E.tensor_tensor(dsc[:, 48:64], dsc[:, 16:32],
                                                        sinqk[:, 0:16], ALU.mult)),
                   reads=[bsin, bdsc], writes=[bdsc])
                em("vector", (lambda E: E.tensor_add(dsc[:, 32:48], dsc[:, 32:48], dsc[:, 48:64])),
                   reads=[bdsc], writes=[bdsc])
                em("vector", (lambda E: E.tensor_scalar_add(dsc[:, 32:48], dsc[:, 32:48], COS_EPS)),
                   reads=[bdsc], writes=[bdsc])
                em("vector", (lambda E: E.reciprocal(dsc[:, 64:80], dsc[:, 32:48])),
                   reads=[bdsc], writes=[bdsc])
                em("vector", (lambda E: E.tensor_tensor(dsc[:, 80:96], cosqk[:, 0:16],
                                                        dsc[:, 64:80], ALU.mult)),
                   reads=[bdsc, bcos], writes=[bdsc])
                em("vector", (lambda E: E.tensor_tensor(dsc[:, 96:112], sinqk[:, 0:16],
                                                        dsc[:, 64:80], ALU.mult)),
                   reads=[bdsc, bsin], writes=[bdsc])
                for hh in range(2):
                    cu_c = PB[3 + hh]
                    cu_s = PB[5 + hh]
                    q0 = hh * 8
                    dst = csin[:, hh * 512:(hh + 1) * 512]
                    tmp = csin[:, 1024 + hh * 512:1024 + (hh + 1) * 512]
                    em("vector", (lambda E, dst=dst, cu_c=cu_c, q0=q0: E.tensor_tensor(
                        dst.rearrange("p (i h) -> p i h", i=8),
                        cu_c[:, :].rearrange("p (i h) -> p i h", i=8),
                        dsc[:, 80 + q0:80 + q0 + 8].broadcast_to([128, 8, 64]), ALU.mult)),
                       reads=[bpb[3 + hh], bdsc, bcsin], writes=[bcsin])
                    em("vector", (lambda E, tmp=tmp, cu_s=cu_s, q0=q0: E.tensor_tensor(
                        tmp.rearrange("p (i h) -> p i h", i=8),
                        cu_s[:, :].rearrange("p (i h) -> p i h", i=8),
                        dsc[:, 96 + q0:96 + q0 + 8].broadcast_to([128, 8, 64]), ALU.mult)),
                       reads=[bpb[5 + hh], bdsc, bcsin], writes=[bcsin])
                    em("vector", (lambda E, dst=dst, tmp=tmp: E.tensor_add(dst, dst, tmp)),
                       reads=[bcsin], writes=[bcsin])
                for g in range(2):
                    pb = PB[3 + g]
                    fns = []
                    for c4 in range(4):
                        c = g * 4 + c4
                        fns.append(lambda E, c=c, c4=c4, pb=pb: E.transpose(
                            pb[:, c4 * 128:(c4 + 1) * 128],
                            csin[:, c * 128:(c + 1) * 128], sI[:, :]))
                    em("tensor", fns, reads=[bcsin, bconsts], writes=[bpb[3 + g]])
                    em("scalar", (lambda E, pb=pb, g=g: E.copy(
                        out=hT_hi[:, g * 512:(g + 1) * 512], in_=pb[:, :])),
                       reads=[bpb[3 + g]], writes=[bcsin])
                    em("vector", (lambda E, pb=pb, g=g: E.tensor_tensor(
                        hT_lo[:, g * 512:(g + 1) * 512], pb[:, :],
                        hT_hi[:, g * 512:(g + 1) * 512], ALU.subtract)),
                       reads=[bpb[3 + g], bcsin], writes=[bcsin])
                for hh in range(2):
                    fns = []
                    for k in range(KC):
                        lh = hT_hi[:, k * 128:(k + 1) * 128]
                        ll = hT_lo[:, k * 128:(k + 1) * 128]
                        fns += mm3_group(PB[5 + hh][:, :], lh, ll,
                                         wo_hi[:, k * 1024 + hh * 512:k * 1024 + (hh + 1) * 512],
                                         wo_lo[:, k * 1024 + hh * 512:k * 1024 + (hh + 1) * 512],
                                         first=(k == 0), last=(k == KC - 1))
                    em("tensor", fns, reads=[bcsin, b_wo], writes=[bpb[5 + hh]])
            def back2b(tt):
                t = t0 + tt
                xt = XRES[:, t * 1024:(t + 1) * 1024]
                em("vector", (lambda E, xt=xt, t=t: E.tensor_scalar(
                    scr[:, :], xt, murs[:, 2 * t:2 * t + 1], murs[:, 2 * t + 1:2 * t + 2],
                    ALU.subtract, ALU.mult)),
                   reads=[bx[t], bmurs], writes=[bscr])
                em("vector", (lambda E: E.tensor_add(scr[:, 0:512], scr[:, 0:512], PB[5][:, :])),
                   reads=[bpb[5]], writes=[bscr])
                em("vector", (lambda E: E.tensor_add(scr[:, 512:1024], scr[:, 512:1024], PB[6][:, :])),
                   reads=[bpb[6]], writes=[bscr])
                em("vector", (lambda E, xt=xt: E.tensor_add(xt, xt, scr[:, :])),
                   reads=[bscr], writes=[bx[t]])
                ln_stats(scr[:, :], [bscr], 4, 5, blnst,
                         sq=csin[:, 1024:2048], sq_buf=bcsin)
                normalize(scr[:, :], bscr, scr[:, :], [bscr], 4, 5, blnst)
                em("vector", (lambda E: E.tensor_mul(scr[:, :], scr[:, :], sln2w)),
                   reads=[bsln], writes=[bscr])
                em("vector", (lambda E: E.tensor_add(scr[:, :], scr[:, :], sln2b)),
                   reads=[bsln], writes=[bscr])

            def back2c(tt):
                transpose_split(
                    scr[:, :], bscr,
                    dst_hi_mk=lambda g, tt=tt: arnb_y2_hi_r[:, g * 4:(g + 1) * 4, tt * 128:(tt + 1) * 128],
                    dst_lo_mk=lambda g, tt=tt: arnb_y2_lo_r[:, g * 4:(g + 1) * 4, tt * 128:(tt + 1) * 128],
                    dst_bufs=[b_y2T], banks=(3, 4))

            if do_att and att_stop >= 1:
                def mid_front_impl(tt):
                    if tt + 1 < tiles_per_seg:
                        front(tt + 1)

                seg_state["mid_front"] = mid_front_impl
                front(0)
                back1(0)
                for tt in range(tiles_per_seg):
                    if tt > 0:
                        back2c(tt - 1)
                    back2(tt)
                    back2b(tt)
                    if tt + 1 < tiles_per_seg:
                        back1(tt + 1)
                back2c(tiles_per_seg - 1)

            q_tok = qtiles * 128
            assert 64 * q_tok <= 16384
            h1_hi = ARNB[:, 0:32 * q_tok]
            h1_lo = ARNB[:, 32 * q_tok:64 * q_tok]
            b_h1 = brole[2]
            toks_w = arena_turnover([b_wv, b_wo, b_wqk])
            w1stg = stg
            b_w1stg = barnc[4]
            stg2 = arnc_f32[:, 4096:4608]
            b_stg2 = barnc[7]
            stgs = (w1stg, stg2)
            b_stgs = (b_w1stg, b_stg2)

            def w1_hi(sl):
                return ARNC[:, sl * 2048: sl * 2048 + 1024]

            def w1_lo(sl):
                return ARNC[:, sl * 2048 + 1024: sl * 2048 + 2048]


            b_w1 = [barnc[0], barnc[1]]

            def w2_hi(sl):
                return ARNC[:, 4096 + sl * 2048: 4096 + sl * 2048 + 1024]

            def w2_lo(sl):
                return ARNC[:, 4096 + sl * 2048 + 1024: 4096 + sl * 2048 + 2048]

            b_w2 = [barnc[2], barnc[6]]
            w2stg = stg
            b_w2stg = barnc[4]
            em("sync", (lambda E: E.dma_start(out=sb2bc, in_=b2bc_h[:, :])),
               writes=[bsln], sem=dBC, inc=16)
            for b in (b_w1[0], b_w1[1], b_w2[0], b_w2[1], b_h1, b_w1stg, b_w2stg):
                b.r.extend(toks_w)

            nquart = tiles_per_seg // qtiles
            for q in (range(nquart) if (do_mlp and do_att) else []):
                qt0 = t0 + q * qtiles
                seg_off = q * qtiles * 128
                def load_w1(o, sl):
                    for wv_w, q_eng, q_sem in ((0, "sync", dW1), (1, "scalar", dW1s)):
                        em(q_eng, (lambda E, o=o, wv_w=wv_w: E.dma_start(
                            out=stgs[wv_w], in_=w1_h[o, :, wv_w * 512:(wv_w + 1) * 512])),
                           writes=[b_stgs[wv_w]], sem=q_sem, inc=16)
                        em("vector", (lambda E, sl=sl, wv_w=wv_w: E.tensor_copy(
                            w1_hi(sl)[:, wv_w * 512:(wv_w + 1) * 512], stgs[wv_w])),
                           reads=[b_stgs[wv_w]], writes=[b_w1[sl]])
                        em("vector", (lambda E, sl=sl, wv_w=wv_w: E.tensor_tensor(
                            w1_lo(sl)[:, wv_w * 512:(wv_w + 1) * 512], stgs[wv_w],
                            w1_hi(sl)[:, wv_w * 512:(wv_w + 1) * 512], ALU.subtract)),
                           reads=[b_stgs[wv_w], b_w1[sl]], writes=[b_w1[sl], b_stgs[wv_w]])

                load_w1(0, 0)
                for o in range(32):
                    sl = o % 2
                    if o + 1 < 32:
                        load_w1(o + 1, 1 - sl)
                    pb = PB[o % 4]
                    pbuf = bpb[o % 4]
                    fns = []
                    for k in range(KC):
                        rh = y2T_hi(k)[:, seg_off:seg_off + q_tok]
                        rl = y2T_lo(k)[:, seg_off:seg_off + q_tok]
                        fns += mm3_group(pb[:, 0:q_tok],
                                         w1_hi(sl)[:, k * 128:(k + 1) * 128],
                                         w1_lo(sl)[:, k * 128:(k + 1) * 128],
                                         rh, rl, first=(k == 0), last=(k == KC - 1))
                    em("tensor", fns, reads=[b_w1[sl], b_y2T], writes=[pbuf])
                    em("scalar", (lambda E, pb=pb, o=o: E.activation(
                        out=scrF[:, 0:q_tok], in_=pb[:, 0:q_tok], func=AF.Gelu_apprx_tanh,
                        bias=sb1c[:, o:o + 1])),
                       reads=[pbuf, bsb], writes=[bscrF])
                    em("scalar", (lambda E, o=o: E.copy(
                        out=h1_hi[:, o * q_tok:(o + 1) * q_tok], in_=scrF[:, 0:q_tok])),
                       reads=[bscrF], writes=[b_h1])
                    em("vector", (lambda E, o=o: E.tensor_tensor(
                        h1_lo[:, o * q_tok:(o + 1) * q_tok], scrF[:, 0:q_tok],
                        h1_hi[:, o * q_tok:(o + 1) * q_tok], ALU.subtract)),
                       reads=[bscrF, b_h1], writes=[b_h1])
                def load_w2(k, sl):
                    for wv_w, q_eng, q_sem in ((0, "sync", dW2), (1, "scalar", dW2s)):
                        em(q_eng, (lambda E, k=k, wv_w=wv_w: E.dma_start(
                            out=stgs[wv_w], in_=w2_h[k * 128:(k + 1) * 128, wv_w * 512:(wv_w + 1) * 512])),
                           writes=[b_stgs[wv_w]], sem=q_sem, inc=16)
                        em("vector", (lambda E, sl=sl, wv_w=wv_w: E.tensor_copy(
                            w2_hi(sl)[:, wv_w * 512:(wv_w + 1) * 512], stgs[wv_w])),
                           reads=[b_stgs[wv_w]], writes=[b_w2[sl]])
                        em("vector", (lambda E, sl=sl, wv_w=wv_w: E.tensor_tensor(
                            w2_lo(sl)[:, wv_w * 512:(wv_w + 1) * 512], stgs[wv_w],
                            w2_hi(sl)[:, wv_w * 512:(wv_w + 1) * 512], ALU.subtract)),
                           reads=[b_stgs[wv_w], b_w2[sl]], writes=[b_w2[sl], b_stgs[wv_w]])

                load_w2(0, 0)
                for k in range(32):
                    sl = k % 2
                    if k + 1 < 32:
                        load_w2(k + 1, 1 - sl)
                    for ti in range(qtiles):
                        lh = h1_hi[:, k * q_tok + ti * 128: k * q_tok + (ti + 1) * 128]
                        ll = h1_lo[:, k * q_tok + ti * 128: k * q_tok + (ti + 1) * 128]
                        for hh in range(2):
                            pb = PB[ti * 2 + hh]
                            fns = mm3_group(pb[:, :], lh, ll,
                                            w2_hi(sl)[:, hh * 512:(hh + 1) * 512],
                                            w2_lo(sl)[:, hh * 512:(hh + 1) * 512],
                                            first=(k == 0), last=(k == 31))
                            em("tensor", fns, reads=[b_h1, b_w2[sl]], writes=[bpb[ti * 2 + hh]])
                for ti in range(qtiles):
                    t = qt0 + ti
                    xt = XRES[:, t * 1024:(t + 1) * 1024]
                    for hh in range(2):
                        em("vector", (lambda E, xt=xt, ti=ti, hh=hh: E.tensor_add(
                            xt[:, hh * 512:(hh + 1) * 512], xt[:, hh * 512:(hh + 1) * 512],
                            PB[ti * 2 + hh][:, :])),
                           reads=[bpb[ti * 2 + hh]], writes=[bx[t]])
                    em("vector", (lambda E, xt=xt: E.tensor_add(xt, xt, sb2bc)),
                       reads=[bsln], writes=[bx[t]])

            toks = arena_turnover([b_wv, b_wo, b_wqk, b_y2T, b_h1, b_stg,
                                   b_w1[0], b_w1[1], b_w2[0], b_w2[1], b_w1stg, b_w2stg])
            for b in barnb + barnc + brole:
                b.r.extend([tk for tk in toks if tk])

        rowbufs = [bcarry] + list(bbrow) + list(bbrowh) + list(bbrowl)
        if "f1" in phases:
            ffn_phase(0)
            _bp2_handoff()
        toks = arena_turnover(barnb + barnc + brole + rowbufs)
        for lst in (barnb, barnc, brole, rowbufs):
            for b in lst:
                b.r.extend(toks)
        if "att" in phases or "mlp" in phases:
            for seg in range(nseg):
                attn_mlp_segment(seg)
        toks = arena_turnover(barnb + barnc + brole + rowbufs)
        for lst in (barnb, barnc, brole, rowbufs):
            for b in lst:
                b.r.extend(toks)
        if "f2" in phases:
            ffn_phase(1)
            _bp2_handoff()

        store_toks = []
        for t in range(NT):
            tok = em("sync", (lambda E, t=t: E.dma_start(
                out=y_h[t * 128:(t + 1) * 128, :], in_=XRES[:, t * 1024:(t + 1) * 1024])),
                reads=[bx[t]], sem=dST, inc=16)
            store_toks.append(tok)
        em("gpsimd", (lambda E: E.engine_nop() if hasattr(E, "engine_nop") else E.memset(tq[:, 0:1], 0.0)),
           extra=[store_toks[-1]])

        with nc.Block() as block:
            @block.sync
            def _(e):
                for cmd in S.q["sync"]:
                    cmd(e)

            @block.gpsimd
            def _(e):
                for cmd in S.q["gpsimd"]:
                    cmd(e)

            @block.tensor
            def _(e):
                for cmd in S.q["tensor"]:
                    cmd(e)

            @block.scalar
            def _(e):
                for cmd in S.q["scalar"]:
                    cmd(e)

            @block.vector
            def _(e):
                for cmd in S.q["vector"]:
                    cmd(e)

    return nc


def _prep(inputs):
    f32 = np.float32
    import ml_dtypes
    bf16 = ml_dtypes.bfloat16
    x = np.asarray(inputs["L0_attn_x"], f32)
    B, T, _ = x.shape
    WQ = np.asarray(inputs["L0_attn_W_Q"], f32)
    WK = np.asarray(inputs["L0_attn_W_K"], f32)
    WV = np.asarray(inputs["L0_attn_W_V"], f32)
    WO = np.asarray(inputs["L0_attn_W_O"], f32)
    W1 = np.asarray(inputs["L0_attn_W1"], f32)
    b1 = np.asarray(inputs["L0_attn_b1"], f32)
    W2 = np.asarray(inputs["L0_attn_W2"], f32)
    b2 = np.asarray(inputs["L0_attn_b2"], f32)
    gw = np.asarray(inputs["L0_ffn_gate_w"], f32)
    gb = np.asarray(inputs["L0_ffn_gate_b"], f32)
    ew = np.asarray(inputs["L0_ffn_expert_w"], f32)
    eb = np.asarray(inputs["L0_ffn_expert_b"], f32)
    ln1w = np.asarray(inputs["L0_attn_ln1_w"], f32)
    ln1b = np.asarray(inputs["L0_attn_ln1_b"], f32)
    ln2w = np.asarray(inputs["L0_attn_ln2_w"], f32)
    ln2b = np.asarray(inputs["L0_attn_ln2_b"], f32)

    C = np.ascontiguousarray
    com = {}
    com["wv"] = C(np.transpose(WV, (1, 0, 2)).reshape(D, NH * DH))
    com["wqk"] = C(np.concatenate([WQ[:, :, 0].T, WK[:, :, 0].T], axis=1))
    com["wo"] = C(np.transpose(WO, (2, 1, 0)).reshape(NH * DH, D))
    com["w1"] = C(W1.reshape(8, 128, 32, 128).transpose(2, 1, 0, 3).reshape(32, 128, 1024))
    com["w2"] = C(W2)
    com["b1c"] = C(b1.reshape(32, 128).T)
    com["b2bc"] = C(np.broadcast_to(b2[None, :], (128, D)))
    expT = np.transpose(ew, (0, 2, 1))
    com["expw"] = C(expT.reshape(NE, 8, 128, 2, 512).transpose(0, 3, 2, 1, 4).reshape(NE, 2, 128, 4096))
    com["expb"] = C(eb)
    com["gw"] = C(gw.T.reshape(8, 128, 8).transpose(1, 0, 2).reshape(128, 64))
    com["gb"] = C(gb.reshape(1, 8))
    com["ln1w"] = C(np.broadcast_to(ln1w[None, :], (128, D)))
    com["ln1b"] = C(np.broadcast_to(ln1b[None, :], (128, D)))
    com["ln2w"] = C(np.broadcast_to(ln2w[None, :], (128, D)))
    com["ln2b"] = C(np.broadcast_to(ln2b[None, :], (128, D)))
    total = float(B * T * 2)
    CONST_VALS = [0.0, 1.0, PI / 2, PI / 4, LN_EPS, COS_EPS, 1.0 / D, 0.5 / total]
    com["consts"] = C(np.tile(np.asarray(CONST_VALS, f32)[None, :], (128, 1)))
    com["ones_col"] = np.ones((128, 1), f32)
    com["ones_col_b"] = np.ones((128, 1), bf16)
    com["ones2"] = np.ones((2, 128), bf16)
    com["u128"] = C(np.triu(np.ones((128, 128), f32)))
    com["ident"] = np.eye(128, dtype=f32)
    shards = [C(x[i]) for i in range(B)]
    return com, shards, B, T


def kernel(**inputs):
    com, shards, B, T = _prep(inputs)
    nc = build_nc(T, n_cores=B)
    in_maps = [dict(com, x=s) for s in shards]
    res = run_bass_kernel_spmd(nc, in_maps, core_ids=list(range(B)))
    out = np.stack([np.asarray(res.results[i]["y"]) for i in range(B)], axis=0)
    return out.astype(np.float32)


# version 21 (final)
# speedup vs baseline: 1.0566x; 1.0413x over previous
import contextlib
import numpy as np
import concourse.bass as bass
import concourse.mybir as mybir
from concourse.bass_utils import run_bass_kernel_spmd

F32 = mybir.dt.float32
BF16 = mybir.dt.bfloat16
AF = mybir.ActivationFunctionType
ALU = mybir.AluOpType
AX = mybir.AxisListType

D = 1024
KC = 8
NH = 16
DH = 64
DMLP = 4096
NE = 8
LN_EPS = 1e-5
COS_EPS = 1e-6
PI = float(np.pi)

ENGINES = ("tensor", "vector", "scalar", "sync", "gpsimd")


class Buf:
    __slots__ = ("w", "r")

    def __init__(self):
        self.w = None
        self.r = []


class Sched:
    def __init__(self):
        self.q = {e: [] for e in ENGINES}
        self.cnt = {}
        self.waited = {e: {} for e in ENGINES}
        self.esem = {}

    def _merge(self, eng, toks):
        m = {}
        for (s, v) in toks:
            if v > m.get(s, -1):
                m[s] = v
        need = []
        for s, v in m.items():
            if self.waited[eng].get(s, -1) < v:
                need.append((s, v))
                self.waited[eng][s] = v
        return need

    def emit(self, eng, fns, reads=(), writes=(), extra=(), sem=None, inc=1):
        if not isinstance(fns, (list, tuple)):
            fns = [fns]
        toks = list(extra)
        for b in reads:
            if b.w:
                toks.append(b.w)
        for b in writes:
            if b.w:
                toks.append(b.w)
            toks.extend(b.r)
        need = self._merge(eng, toks)
        if sem is None:
            sem = self.esem[eng]
        self.cnt[sem] = self.cnt.get(sem, 0) + inc
        tok = (sem, self.cnt[sem])

        def cmd(E, fns=tuple(fns), need=tuple(need), sem=sem, inc=inc):
            for s, v in need:
                E.wait_ge(s, v)
            ins = None
            for f in fns:
                ins = f(E)
            ins.then_inc(sem, inc)

        self.q[eng].append(cmd)
        for b in reads:
            b.r.append(tok)
        for b in writes:
            b.w = tok
            b.r = []
        return tok

    def barrier_tokens(self):
        return [(s, v) for s, v in self.cnt.items()]


def build_nc(T, n_cores=8, phases=("f1", "att", "mlp", "f2"), ffn_parts=("a", "cc", "b"), att_stop=99, no_affine=False):
    NT = T // 128
    assert T % 256 == 0
    nseg = 2 if NT >= 8 else 1
    tiles_per_seg = NT // nseg
    qtiles = 2 if tiles_per_seg % 2 == 0 else 1
    nc = bass.Bass("TRN2", target_bir_lowering=False, debug=False)

    dram = {}

    def din(name, shape, dt=F32):
        dram[name] = nc.dram_tensor(name, list(shape), dt, kind="ExternalInput").ap()
        return dram[name]

    x_h = din("x", (T, D))
    wv_h = din("wv", (D, NH * DH))
    wqk_h = din("wqk", (D, 2 * NH))
    wo_h = din("wo", (NH * DH, D))
    w1_h = din("w1", (32, 128, 1024))
    w2_h = din("w2", (DMLP, D))
    b1c_h = din("b1c", (128, 32))
    b2bc_h = din("b2bc", (128, D))
    expw_h = din("expw", (NE, 2, 128, 4096))
    expb_h = din("expb", (NE, D))
    gw_h = din("gw", (128, 8 * 8))
    gb_h = din("gb", (1, 8))
    ln1w_h = din("ln1w", (128, D))
    ln1b_h = din("ln1b", (128, D))
    ln2w_h = din("ln2w", (128, D))
    ln2b_h = din("ln2b", (128, D))
    consts_h = din("consts", (128, 8))
    onesc_h = din("ones_col", (128, 1))
    onescb_h = din("ones_col_b", (128, 1), BF16)
    onesr_h = din("ones_row", (1, 128))
    ones2_h = din("ones2", (2, 128), BF16)
    u128_h = din("u128", (128, 128))
    ident_h = din("ident", (128, 128))

    y_h = nc.dram_tensor("y", [T, D], F32, kind="ExternalOutput").ap()

    cc_bufs = []
    for i in range(2):
        cc_bufs.append((nc.dram_tensor(f"cc_in{i}", [1, 8], F32),
                        nc.dram_tensor(f"cc_out{i}", [1, 8], F32)))

    S = Sched()

    with contextlib.ExitStack() as ctx:
        E = ctx.enter_context

        XRES = E(nc.sbuf_tensor("XRES", [128, NT * 1024], F32))
        arnb_size = max(16 * T, 16384 + 16 * (T // nseg))
        ARNB = E(nc.sbuf_tensor("ARNB", [128, arnb_size], BF16))
        ARNC = E(nc.sbuf_tensor("ARNC", [128, 17920], BF16))
        scr = E(nc.sbuf_tensor("scr", [128, 1024], F32))
        scr2 = E(nc.sbuf_tensor("scr2", [128, 1024], F32))
        _scr2_b16 = scr2[:, :].bitcast(BF16)
        bp2 = (_scr2_b16[0:2, 0:512], _scr2_b16[0:2, 512:1024])
        scrF = E(nc.sbuf_tensor("scrF", [128, 256], F32))
        csin = E(nc.sbuf_tensor("csin", [128, 2080], F32))
        ROWS = E(nc.sbuf_tensor("ROWS", [1, 2080], F32))
        carry = ROWS[0:1, 0:2080]
        _rows_b16 = ROWS[0:1, :].bitcast(BF16)
        tq = E(nc.sbuf_tensor("tq", [128, 32], F32))
        cosqk = E(nc.sbuf_tensor("cosqk", [128, 32], F32))
        sinqk = E(nc.sbuf_tensor("sinqk", [128, 32], F32))
        dsc = E(nc.sbuf_tensor("dsc", [128, 80], F32))
        murs = E(nc.sbuf_tensor("murs", [128, 2 * NT], F32))
        lnst = E(nc.sbuf_tensor("lnst", [128, 8], F32))
        gate = E(nc.sbuf_tensor("gate", [128, 8], F32))
        top8 = E(nc.sbuf_tensor("top8", [128, 8], F32))
        vind = E(nc.sbuf_tensor("vind", [128, 8], BF16))
        cntb = E(nc.sbuf_tensor("cntb", [128, 8], F32))
        cntr = E(nc.sbuf_tensor("cntr", [1, 8], F32))
        brow = (ROWS[0:1, 0:512], ROWS[0:1, 1024:1536])
        browh = (_rows_b16[0:1, 1024:1536], _rows_b16[0:1, 3072:3584])
        browl = (_rows_b16[0:1, 1536:2048], _rows_b16[0:1, 3584:4096])
        gbh = E(nc.sbuf_tensor("gbh", [1, 8], BF16))
        gbl = E(nc.sbuf_tensor("gbl", [1, 8], BF16))
        gbr = E(nc.sbuf_tensor("gbr", [1, 8], F32))
        sconst = E(nc.sbuf_tensor("sconst", [128, 8], F32))
        sonesc = E(nc.sbuf_tensor("sonesc", [128, 1], F32))
        sonescb = E(nc.sbuf_tensor("sonescb", [128, 1], BF16))
        sonesr = E(nc.sbuf_tensor("sonesr", [1, 128], F32))
        sones2 = E(nc.sbuf_tensor("sones2", [2, 128], BF16))
        sU = E(nc.sbuf_tensor("sU", [128, 128], F32))
        sI = E(nc.sbuf_tensor("sI", [128, 128], F32))
        gwst = E(nc.sbuf_tensor("gwst", [128, 64], F32))
        gwhi = E(nc.sbuf_tensor("gwhi", [128, 64], BF16))
        gwlo = E(nc.sbuf_tensor("gwlo", [128, 64], BF16))
        BC = E(nc.sbuf_tensor("BC", [128, 4 * D], F32))
        sln1w = BC[:, 0:D]
        sln1b = BC[:, D:2 * D]
        sln2w = BC[:, 2 * D:3 * D]
        sln2b = BC[:, 3 * D:4 * D]
        sb2bc = BC[:, 0:D]
        sb1c = E(nc.sbuf_tensor("sb1c", [128, 32], F32))

        cnt_scale = 0.5 / float(n_cores * T * 2)
        CONST_VALS = [0.0, 1.0, PI / 2, PI / 4, LN_EPS, COS_EPS, 1.0 / D, cnt_scale]
        for j, v in enumerate(CONST_VALS):
            nc.const_aps.register(F32, float(v), sconst[:, j:j + 1])

        PB = [E(nc.psum_tensor(f"PB{i}", [128, 512], F32)) for i in range(8)]

        stn = E(nc.semaphore("stn"))
        vec = E(nc.semaphore("vec"))
        scl = E(nc.semaphore("scl"))
        gps = E(nc.semaphore("gps"))
        dIN = E(nc.semaphore("dIN"))
        dX = [E(nc.semaphore(f"dX{i}")) for i in range(4)]
        dE0 = E(nc.semaphore("dE0"))
        dE1 = E(nc.semaphore("dE1"))
        dB = (E(nc.semaphore("dB0")), E(nc.semaphore("dB1")))
        dBc = (E(nc.semaphore("dBc0")), E(nc.semaphore("dBc1")))
        dWV = E(nc.semaphore("dWV"))
        dWVs = E(nc.semaphore("dWVs"))
        dWOs = E(nc.semaphore("dWOs"))
        dWO = E(nc.semaphore("dWO"))
        dWQ = E(nc.semaphore("dWQ"))
        dW1 = E(nc.semaphore("dW1"))
        dW2 = E(nc.semaphore("dW2"))
        dST = E(nc.semaphore("dST"))
        dCC = E(nc.semaphore("dCC"))
        dC0 = E(nc.semaphore("dC0"))
        dC1 = E(nc.semaphore("dC1"))
        dWVc = E(nc.semaphore("dWVc"))
        dWOc = E(nc.semaphore("dWOc"))
        dWQc = E(nc.semaphore("dWQc"))
        dW1c = (E(nc.semaphore("dW1c0")), E(nc.semaphore("dW1c1")))
        dW2c = (E(nc.semaphore("dW2c0")), E(nc.semaphore("dW2c1")))
        dW1s = E(nc.semaphore("dW1s"))
        dW2s = E(nc.semaphore("dW2s"))
        dBC = E(nc.semaphore("dBC"))
        dCR = E(nc.semaphore("dCR"))
        S.esem.update(tensor=stn, vector=vec, scalar=scl, gpsimd=gps, sync=None)

        bx = [Buf() for _ in range(NT)]
        bscr, bscr2, bscrF = Buf(), Buf(), Buf()
        bbp2 = (Buf(), Buf())
        bcsin, bcarry = Buf(), Buf()
        btq, bcos, bsin, bdsc = Buf(), Buf(), Buf(), Buf()
        bmurs = Buf()
        blnst = Buf()
        bgate, btop8, bvind, bcntb, bcntr = Buf(), Buf(), Buf(), Buf(), Buf()
        bbrow = (Buf(), Buf())
        bbrowh = (Buf(), Buf())
        bbrowl = (Buf(), Buf())
        bgb = Buf()
        bgw = Buf()
        bconsts = Buf()
        bpb = [Buf() for _ in range(8)]
        barnb = [Buf() for _ in range(NT)]
        barnc = [Buf() for _ in range(8)]
        brole = [Buf() for _ in range(4)]
        bsln = Buf()
        bsb = Buf()

        arnb_f32 = ARNB[:, :].bitcast(F32)
        arnc_f32 = ARNC[:, :].bitcast(F32)

        def arena_turnover(bufs):
            toks = []
            for b in bufs:
                if b.w:
                    toks.append(b.w)
                toks.extend(b.r)
                b.w = None
                b.r = []
            return toks

        em = S.emit

        ng = max(1, NT // 4)
        for t in range(NT):
            em("sync", (lambda E, t=t: E.dma_start(
                out=XRES[:, t * 1024:(t + 1) * 1024], in_=x_h[t * 128:(t + 1) * 128, :])),
                writes=[bx[t]], sem=dX[t // ng], inc=16)
        for g in range(4):
            lo, hi = g * ng, min((g + 1) * ng, NT)
            if lo >= NT:
                break
            tok = (dX[g], S.cnt[dX[g]])
            for t in range(lo, hi):
                bx[t].w = tok
        for hb, sb, b in ((consts_h, sconst, bconsts), (onesc_h, sonesc, bconsts),
                          (onescb_h, sonescb, bconsts), (onesr_h, sonesr, bconsts),
                          (ones2_h, sones2, bconsts),
                          (u128_h, sU, bconsts), (ident_h, sI, bconsts),
                          (gw_h, gwst, bgw), (gb_h, gbr, bgb),
                          (b1c_h, sb1c, bsb)):
            em("sync", (lambda E, hb=hb, sb=sb: E.dma_start(out=sb[:, :], in_=hb[:, :])),
               writes=[b], sem=dIN, inc=16)
        _init_total = (dIN, S.cnt[dIN])
        for b in [bconsts, bgw, bgb, bsln, bsb]:
            b.w = _init_total

        em("scalar", (lambda E: E.copy(out=gwhi[:, :], in_=gwst[:, :])), reads=[bgw], writes=[bgw])
        em("vector", (lambda E: E.tensor_sub(gwlo[:, :], gwst[:, :], gwhi[:, :])),
           reads=[bgw], writes=[bgw])
        em("scalar", (lambda E: E.copy(out=gbh[:, :], in_=gbr[:, :])), reads=[bgb], writes=[bgb])
        em("vector", (lambda E: E.tensor_sub(gbl[:, :], gbr[:, :], gbh[:, :])),
           reads=[bgb], writes=[bgb])

        def ln_stats(src_ap, src_bufs, mu_slot, r_slot, dst_buf, keep_vpe=None,
                     sq=None, sq_buf=None):
            s1 = lnst[:, mu_slot:mu_slot + 1]
            s2 = lnst[:, r_slot:r_slot + 1]
            t3 = lnst[:, 6:7]
            t4 = lnst[:, 7:8]
            em("vector", (lambda E: E.reduce_sum(s1, src_ap, axis=AX.X)),
               reads=src_bufs, writes=[dst_buf])
            if sq is None:
                sq, sq_buf = scr2[:, :], bscr2
            em("scalar", (lambda E, sq=sq: E.activation(out=sq, in_=src_ap, func=AF.Square)),
               reads=src_bufs, writes=[sq_buf])
            em("vector", (lambda E, sq=sq: E.reduce_sum(s2, sq, axis=AX.X)),
               reads=[sq_buf], writes=[dst_buf])
            em("vector", (lambda E: E.tensor_scalar_mul(s1, s1, 1.0 / D)),
               reads=[dst_buf], writes=[dst_buf])
            em("vector", (lambda E: E.tensor_scalar_mul(s2, s2, 1.0 / D)),
               reads=[dst_buf], writes=[dst_buf])
            em("vector", (lambda E: E.tensor_mul(t3, s1, s1)), reads=[dst_buf], writes=[dst_buf])
            em("vector", (lambda E: E.tensor_sub(s2, s2, t3)), reads=[dst_buf], writes=[dst_buf])
            em("vector", (lambda E: E.tensor_scalar_add(s2, s2, LN_EPS)),
               reads=[dst_buf], writes=[dst_buf])
            if keep_vpe is not None:
                em("vector", (lambda E: E.tensor_copy(lnst[:, keep_vpe:keep_vpe + 1], s2)),
                   reads=[dst_buf], writes=[dst_buf])
            em("scalar", (lambda E: E.activation(out=t3, in_=s2, func=AF.Sqrt)),
               reads=[dst_buf], writes=[dst_buf])
            em("vector", (lambda E: E.reciprocal(t4, t3)), reads=[dst_buf], writes=[dst_buf])
            em("vector", (lambda E: E.tensor_mul(t3, t4, t4)), reads=[dst_buf], writes=[dst_buf])
            em("vector", (lambda E: E.tensor_mul(t3, t3, s2)), reads=[dst_buf], writes=[dst_buf])
            em("vector", (lambda E: E.tensor_scalar(t3, t3, -0.5, 1.5, ALU.mult, ALU.add)),
               reads=[dst_buf], writes=[dst_buf])
            em("vector", (lambda E: E.tensor_mul(s2, t4, t3)), reads=[dst_buf], writes=[dst_buf])

        def normalize(dst_ap, dst_buf, src_ap, src_bufs, mu_slot, r_slot, stats_buf):
            em("vector", (lambda E: E.tensor_scalar(
                dst_ap, src_ap, lnst[:, mu_slot:mu_slot + 1], lnst[:, r_slot:r_slot + 1],
                ALU.subtract, ALU.mult)),
               reads=list(src_bufs) + [stats_buf], writes=[dst_buf])

        def transpose_split(src, src_buf, dst_hi_mk, dst_lo_mk, dst_bufs,
                            affine=None, banks=(0, 1)):
            for g in range(2):
                bk = banks[g]
                pb = PB[bk]
                fns = []
                for c4 in range(4):
                    c = g * 4 + c4
                    fns.append(lambda E, c=c, c4=c4, pb=pb: E.transpose(
                        pb[:, c4 * 128:(c4 + 1) * 128], src[:, c * 128:(c + 1) * 128], sI[:, :]))
                ttok = em("tensor", fns, reads=[src_buf, bconsts], writes=[bpb[bk]])
                if affine is None:
                    hi_ap = dst_hi_mk(g)
                    lo_ap = dst_lo_mk(g)
                    em("scalar", (lambda E, pb=pb, hi_ap=hi_ap: E.copy(out=hi_ap, in_=pb[:, :])),
                       reads=[bpb[bk]], writes=dst_bufs)
                    em("vector", (lambda E, pb=pb, hi_ap=hi_ap, lo_ap=lo_ap: E.tensor_tensor(
                        lo_ap, pb[:, :].rearrange("p (c n) -> p c n", c=4), hi_ap, ALU.subtract)),
                       reads=[bpb[bk]] + list(dst_bufs), writes=dst_bufs)
                else:
                    wc, bc, abuf = affine
                    for c4 in range(4):
                        c = g * 4 + c4
                        sl = pb[:, c4 * 128:(c4 + 1) * 128]
                        hi_ap = dst_hi_mk(c)
                        lo_ap = dst_lo_mk(c)
                        em("scalar", (lambda E, sl=sl, hi_ap=hi_ap, c=c, wc=wc, bc=bc:
                                      E.activation(out=hi_ap, in_=sl, func=AF.Identity,
                                                   bias=bc[:, c:c + 1], scale=wc[:, c:c + 1])),
                           reads=[bpb[g], abuf], writes=dst_bufs)
                        em("vector", (lambda E, sl=sl, c=c, wc=wc, bc=bc: E.scalar_tensor_tensor(
                            scrF[:, 0:128], sl, wc[:, c:c + 1],
                            bc[:, c:c + 1].broadcast_to([128, 128]),
                            ALU.mult, ALU.add)),
                           reads=[bpb[g], abuf], writes=[bscrF])
                        em("vector", (lambda E, hi_ap=hi_ap, lo_ap=lo_ap: E.tensor_tensor(
                            lo_ap, scrF[:, 0:128], hi_ap, ALU.subtract)),
                           reads=[bscrF] + list(dst_bufs), writes=dst_bufs)

        def mm3_group(pb_ap, lhs_hi, lhs_lo, rhs_hi, rhs_lo, first, last):
            return [
                (lambda E: E.matmul(pb_ap, lhs_hi, rhs_hi, start=first, stop=False)),
                (lambda E: E.matmul(pb_ap, lhs_hi, rhs_lo, start=False, stop=False)),
                (lambda E: E.matmul(pb_ap, lhs_lo, rhs_hi, start=False, stop=last)),
            ]

        def xnT_hi(c):
            return ARNB[:, c * T:(c + 1) * T]

        def xnT_lo(c):
            return ARNB[:, NT * 1024 + c * T: NT * 1024 + (c + 1) * T]

        arnb_r = ARNB[:, 0:16 * T].rearrange("p (c n) -> p c n", c=2 * KC)

        def ffn_phase(call_idx):
            cc_in, cc_out = cc_bufs[call_idx]
            do_a = "a" in ffn_parts
            seq2 = [(e, h) for e in range(NE) for h in range(2)]
            do_cc = "cc" in ffn_parts
            do_b = "b" in ffn_parts
            pa_buf = (scr, scr2)
            pa_bbuf = (bscr, bscr2)

            def pa_s1(t):
                xt = XRES[:, t * 1024:(t + 1) * 1024]
                b = pa_buf[t % 2]
                bb = pa_bbuf[t % 2]
                ln_stats(xt, [bx[t]], 0, 1, blnst, sq=b[:, :], sq_buf=bb)
                normalize(b[:, :], bb, xt, [bx[t]], 0, 1, blnst)

            def pa_s2(t):
                b = pa_buf[t % 2]
                bb = pa_bbuf[t % 2]
                transpose_split(
                    b[:, :], bb,
                    dst_hi_mk=lambda g, t=t: arnb_r[:, g * 4:(g + 1) * 4, t * 128:(t + 1) * 128],
                    dst_lo_mk=lambda g, t=t: arnb_r[:, KC + g * 4:KC + (g + 1) * 4, t * 128:(t + 1) * 128],
                    dst_bufs=[barnb[t]])
                gpb = PB[2 + (t % 2)]
                gbuf = bpb[2 + (t % 2)]
                fns = []
                for k in range(KC):
                    lh = xnT_hi(k)[:, t * 128:(t + 1) * 128]
                    ll = xnT_lo(k)[:, t * 128:(t + 1) * 128]
                    fns += mm3_group(gpb[:, 0:8], lh, ll,
                                     gwhi[:, k * 8:(k + 1) * 8], gwlo[:, k * 8:(k + 1) * 8],
                                     first=(k == 0), last=False)
                fns.append(lambda E, gpb=gpb: E.matmul(gpb[:, 0:8], sones2[0:1, :], gbh[0:1, :],
                                                       start=False, stop=False))
                fns.append(lambda E, gpb=gpb: E.matmul(gpb[:, 0:8], sones2[0:1, :], gbl[0:1, :],
                                                       start=False, stop=True))
                em("tensor", fns, reads=[barnb[t], bgw, bgb, bconsts], writes=[gbuf])

            def pa_s3(t):
                gpb = PB[2 + (t % 2)]
                gbuf = bpb[2 + (t % 2)]
                em("vector", (lambda E, gpb=gpb: E.tensor_copy(gate[:, :], gpb[:, 0:8])),
                   reads=[gbuf], writes=[bgate])
                em("vector", (lambda E: E.max(top8[:, :], gate[:, :])),
                   reads=[bgate], writes=[btop8])
                em("vector", (lambda E: E.tensor_single_scalar(
                    vind[:, :], gate[:, :], top8[:, 1:2], ALU.is_ge)),
                   reads=[bgate, btop8], writes=[bvind])
                em("tensor", (lambda E, t=t: E.matmul(PB[4][0:1, 0:8], sonescb[:, :], vind[:, :],
                                                      start=(t == 0), stop=(t == NT - 1))),
                   reads=[bvind, bconsts],
                   writes=([bpb[4]] if t in (0, NT - 1) else []))

            _passA_emit = [None]

            def _run_passA():
                if do_b:
                    load_expert_weights(*seq2[0], 0)
                pa_s1(0)
                for t in range(NT):
                    pa_s2(t)
                    if t + 1 < NT:
                        pa_s1(t + 1)
                    pa_s3(t)
            if not (do_a and do_cc and do_b):
                if do_a:
                    _run_passA()
                return

            def _run_counts():
                em("scalar", (lambda E: E.copy(out=cntr[:, :], in_=PB[4][0:1, 0:8])),
                   reads=[bpb[4]], writes=[bcntr])
                em("gpsimd", (lambda E: E.dma_start(out=cc_in[:, :], in_=cntr[:, :])),
                   reads=[bcntr], sem=dCC, inc=16)
                cc_tok = em("gpsimd", (lambda E: E.collective_compute(
                    "AllReduce", ALU.add, replica_groups=[list(range(8))],
                    ins=[cc_in[:, :]], outs=[cc_out[:, :]])),
                   extra=[(dCC, S.cnt.get(dCC, 0))])
                em("gpsimd", (lambda E: E.dma_start(out=cntr[:, :], in_=cc_out[:, :])),
                   writes=[bcntr], extra=[cc_tok], sem=dCC, inc=16)
                em("tensor", (lambda E: E.matmul(PB[5][:, 0:8], sonesr[0:1, :], cntr[0:1, :],
                                                 start=True, stop=True)),
                   reads=[bcntr, bconsts], writes=[bpb[5]])
                em("scalar", (lambda E: E.activation(out=cntb[:, :], in_=PB[5][:, 0:8],
                                                     func=AF.Identity, scale=cnt_scale)),
                   reads=[bpb[5]], writes=[bcntb])

            stage_f = arnc_f32[:, 0:512]
            b_stage = barnc[0]

            def wem_hi(s):
                return ARNC[:, 1024 + s * 8192: 1024 + s * 8192 + 4096]

            def wem_lo(s):
                return ARNC[:, 5120 + s * 8192: 5120 + s * 8192 + 4096]

            b_wem = [barnc[1], barnc[2]]

            dE = (dE0, dE1)

            def load_expert_weights(e, half, s):
                em("gpsimd", (lambda E, e=e, half=half, s=s: E.dma_start(
                    out=wem_hi(s), in_=expw_h[e, half, :, :])),
                   writes=[b_wem[s]], sem=(dC0, dC1)[s], inc=16)
                for wave in range(8):
                    em("sync", (lambda E, e=e, half=half, wave=wave: E.dma_start(
                        out=stage_f[:, :],
                        in_=expw_h[e, half, :, wave * 512:(wave + 1) * 512])),
                       writes=[b_stage], sem=dE[s], inc=16)
                    em("vector", (lambda E, s=s, wave=wave: E.tensor_tensor(
                        wem_lo(s)[:, wave * 512:(wave + 1) * 512],
                        stage_f[:, :],
                        wem_hi(s)[:, wave * 512:(wave + 1) * 512], ALU.subtract)),
                       reads=[b_stage, b_wem[s]], writes=[b_wem[s], b_stage])

            def load_expert_cast(e, half, s):
                em("gpsimd", (lambda E, e=e, half=half, s=s: E.dma_start(
                    out=wem_hi(s), in_=expw_h[e, half, :, :])),
                   writes=[b_wem[s]], sem=(dC0, dC1)[s], inc=16)

            def load_expert_wave(e, half, s, wave):
                em("sync", (lambda E, e=e, half=half, wave=wave: E.dma_start(
                    out=stage_f[:, :],
                    in_=expw_h[e, half, :, wave * 512:(wave + 1) * 512])),
                   writes=[b_stage], sem=dE[s], inc=16)
                em("vector", (lambda E, s=s, wave=wave: E.tensor_tensor(
                    wem_lo(s)[:, wave * 512:(wave + 1) * 512],
                    stage_f[:, :],
                    wem_hi(s)[:, wave * 512:(wave + 1) * 512], ALU.subtract)),
                   reads=[b_stage, b_wem[s]], writes=[b_wem[s], b_stage])

            def load_expert_bias(e, half, s):
                em("sync", (lambda E, e=e, half=half, s=s: E.dma_start(
                    out=brow[s], in_=expb_h[e:e + 1, half * 512:(half + 1) * 512])),
                   writes=[bbrow[s]], sem=dB[s], inc=16)
                em("scalar", (lambda E, s=s: E.copy(out=bp2[s][0:1, :], in_=brow[s])),
                   reads=[bbrow[s]], writes=[bbp2[s]])
                em("vector", (lambda E, s=s: E.tensor_sub(browl[s], brow[s], bp2[s][0:1, :])),
                   reads=[bbrow[s], bbp2[s]], writes=[bbrowl[s]])
                em("gpsimd", (lambda E, s=s: E.dma_start(out=bp2[s][1:2, :], in_=browl[s])),
                   reads=[bbrowl[s]], writes=[bbp2[s]], sem=dBc[s], inc=16)

            for b in bbp2:
                if bscr2.w:
                    b.r.append(bscr2.w)
                b.r.extend(bscr2.r)
            seq = [(e, h) for e in range(NE) for h in range(2)]
            if do_a:
                _run_passA()
            else:
                load_expert_weights(*seq[0], 0)
            _run_counts()
            load_expert_bias(*seq[0], 0)
            for i, (e, half) in enumerate(seq):
                s = i % 2
                if i + 1 < len(seq):
                    load_expert_cast(*seq[i + 1], 1 - s)
                for t in range(NT):
                    if i + 1 < len(seq):
                        for wv in range(t * 8 // NT, (t + 1) * 8 // NT):
                            load_expert_wave(*seq[i + 1], 1 - s, wv)
                        if t == min(1, NT - 1):
                            load_expert_bias(*seq[i + 1], 1 - s)
                    pb = PB[t % 4]
                    pbuf = bpb[t % 4]
                    fns = []
                    for k in range(KC):
                        lh = xnT_hi(k)[:, t * 128:(t + 1) * 128]
                        ll = xnT_lo(k)[:, t * 128:(t + 1) * 128]
                        fns += mm3_group(pb[:, :], lh, ll,
                                         wem_hi(s)[:, k * 512:(k + 1) * 512],
                                         wem_lo(s)[:, k * 512:(k + 1) * 512],
                                         first=(k == 0), last=False)
                    fns.append(lambda E, pb=pb, s=s: E.matmul(pb[:, :], sones2[:, :], bp2[s][:, :],
                                                              start=False, stop=True))
                    em("tensor", fns,
                       reads=[barnb[t], b_wem[s], bbp2[s], bconsts], writes=[pbuf])
                    half_ap = scr[:, (t % 2) * 512:(t % 2) * 512 + 512]
                    em("scalar", (lambda E, pb=pb, half_ap=half_ap, e=e: E.activation(
                        out=half_ap, in_=pb[:, :], func=AF.Relu, scale=cntb[:, e:e + 1])),
                       reads=[pbuf, bcntb], writes=[bscr])
                    em("vector", (lambda E, t=t, half=half, half_ap=half_ap: E.tensor_add(
                        XRES[:, t * 1024 + half * 512: t * 1024 + half * 512 + 512],
                        XRES[:, t * 1024 + half * 512: t * 1024 + half * 512 + 512],
                        half_ap)),
                       reads=[bscr], writes=[bx[t]])

        seg_T = T // nseg

        def _bp2_handoff():
            toks = arena_turnover(list(bbp2))
            bscr2.r.extend(toks)

        def attn_mlp_segment(seg):
            do_att = "att" in phases
            do_mlp = "mlp" in phases
            t0 = seg * tiles_per_seg
            wo_hi = ARNC[:, 0:8192]
            wo_lo = ARNC[:, 8192:16384]
            stg = arnc_f32[:, 8192:8704]
            wqk_hi = ARNC[:, 17408:17664]
            wqk_lo = ARNC[:, 17664:17920]
            b_wo, b_stg, b_wqk = barnc[3], barnc[4], barnc[5]
            wv_hi = ARNB[:, 0:8192]
            wv_lo = ARNB[:, 8192:16384]
            b_wv = brole[0]
            stgB_att = ARNB[:, 16384:17408].bitcast(F32)
            b_stgB_att = brole[3]

            def y2T_hi(c):
                return ARNB[:, 16384 + c * seg_T: 16384 + (c + 1) * seg_T]

            def y2T_lo(c):
                return ARNB[:, 16384 + KC * seg_T + c * seg_T: 16384 + KC * seg_T + (c + 1) * seg_T]

            arnb_y2_hi_r = ARNB[:, 16384:16384 + KC * seg_T].rearrange("p (c n) -> p c n", c=KC)
            arnb_y2_lo_r = ARNB[:, 16384 + KC * seg_T:16384 + 2 * KC * seg_T].rearrange("p (c n) -> p c n", c=KC)

            b_y2T = brole[1]

            em("gpsimd", (lambda E: E.dma_start(
                out=wv_hi.rearrange("p (c n) -> p c n", c=KC),
                in_=wv_h[:, :].rearrange("(c p) n -> p c n", p=128))),
               writes=[b_wv], sem=dWVc, inc=16)
            for k in range(KC):
                for wv_w, stq, stb, q_eng, q_sem in ((0, stg, b_stg, "sync", dWV),
                                                     (1, None, None, "scalar", dWVs)):
                    if wv_w == 1:
                        stq, stb = stgB_att, b_stgB_att
                    em(q_eng, (lambda E, k=k, wv_w=wv_w, stq=stq: E.dma_start(
                        out=stq, in_=wv_h[k * 128:(k + 1) * 128, wv_w * 512:(wv_w + 1) * 512])),
                       writes=[stb], sem=q_sem, inc=16)
                    em("vector", (lambda E, k=k, wv_w=wv_w, stq=stq: E.tensor_tensor(
                        wv_lo[:, k * 1024 + wv_w * 512:k * 1024 + (wv_w + 1) * 512], stq,
                        wv_hi[:, k * 1024 + wv_w * 512:k * 1024 + (wv_w + 1) * 512], ALU.subtract)),
                       reads=[stb, b_wv], writes=[b_wv, stb])
            em("gpsimd", (lambda E: E.dma_start(
                out=wo_hi.rearrange("p (c n) -> p c n", c=KC),
                in_=wo_h[:, :].rearrange("(c p) n -> p c n", p=128))),
               writes=[b_wo], sem=dWOc, inc=16)
            for k in range(KC):
                for wo_w in range(2):
                    stq, stb = (stg, b_stg) if wo_w == 0 else (stgB_att, b_stgB_att)
                    q_eng, q_sem = ("sync", dWO) if wo_w == 0 else ("scalar", dWOs)
                    em(q_eng, (lambda E, k=k, wo_w=wo_w, stq=stq: E.dma_start(
                        out=stq, in_=wo_h[k * 128:(k + 1) * 128, wo_w * 512:(wo_w + 1) * 512])),
                       writes=[stb], sem=q_sem, inc=16)
                    em("vector", (lambda E, k=k, wo_w=wo_w, stq=stq: E.tensor_tensor(
                        wo_lo[:, k * 1024 + wo_w * 512:k * 1024 + (wo_w + 1) * 512], stq,
                        wo_hi[:, k * 1024 + wo_w * 512:k * 1024 + (wo_w + 1) * 512], ALU.subtract)),
                       reads=[stb, b_wo], writes=[b_wo, stb])
            em("gpsimd", (lambda E: E.dma_start(
                out=wqk_hi.rearrange("p (c n) -> p c n", c=KC),
                in_=wqk_h[:, :].rearrange("(c p) n -> p c n", p=128))),
               writes=[b_wqk], sem=dWQc, inc=16)
            em("sync", (lambda E: E.dma_start(
                out=stg[:, 0:256].rearrange("p (c n) -> p c n", c=KC),
                in_=wqk_h[:, :].rearrange("(c p) n -> p c n", p=128))),
               writes=[b_stg], sem=dWQ, inc=16)
            em("vector", (lambda E: E.tensor_tensor(wqk_lo, stg[:, 0:256], wqk_hi, ALU.subtract)),
               reads=[b_stg, b_wqk], writes=[b_wqk, b_stg])
            for bc_h, bc_dst in ((ln1w_h, sln1w), (ln1b_h, sln1b),
                                 (ln2w_h, sln2w), (ln2b_h, sln2b)):
                em("sync", (lambda E, bc_h=bc_h, bc_dst=bc_dst: E.dma_start(
                    out=bc_dst, in_=bc_h[:, :])), writes=[bsln], sem=dBC, inc=16)
            _stgB_toks = arena_turnover([b_stgB_att])
            b_y2T.r.extend(_stgB_toks)

            scr2_b = scr2[:, :].bitcast(BF16)
            yT_hi = scr2_b[:, 0:1024]
            yT_lo = scr2_b[:, 1024:2048]
            b_yT = bscr2

            hT_hi = csin[:, :].bitcast(BF16)[:, 2048:3072]
            hT_lo = csin[:, :].bitcast(BF16)[:, 3072:4096]

            def front(tt):
                t = t0 + tt
                xt = XRES[:, t * 1024:(t + 1) * 1024]
                ln_stats(xt, [bx[t]], 0, 1, blnst, keep_vpe=2)
                em("vector", (lambda E, t=t: E.tensor_copy(murs[:, 2 * t:2 * t + 2], lnst[:, 0:2])),
                   reads=[blnst], writes=[bmurs])
                normalize(scr[:, :], bscr, xt, [bx[t]], 0, 1, blnst)
                t6 = lnst[:, 6:7]
                t7 = lnst[:, 7:8]
                r1 = lnst[:, 3:4]
                em("vector", (lambda E: E.tensor_mul(t6, lnst[:, 1:2], lnst[:, 1:2])),
                   reads=[blnst], writes=[blnst])
                em("vector", (lambda E: E.tensor_scalar_add(t7, lnst[:, 2:3], -LN_EPS)),
                   reads=[blnst], writes=[blnst])
                em("vector", (lambda E: E.tensor_mul(t6, t6, t7)),
                   reads=[blnst], writes=[blnst])
                em("vector", (lambda E: E.tensor_scalar_add(t6, t6, LN_EPS)),
                   reads=[blnst], writes=[blnst])
                em("scalar", (lambda E: E.activation(out=t7, in_=t6, func=AF.Sqrt)),
                   reads=[blnst], writes=[blnst])
                em("vector", (lambda E: E.reciprocal(r1, t7)),
                   reads=[blnst], writes=[blnst])
                em("vector", (lambda E: E.tensor_mul(t7, r1, r1)),
                   reads=[blnst], writes=[blnst])
                em("vector", (lambda E: E.tensor_mul(t7, t7, t6)),
                   reads=[blnst], writes=[blnst])
                em("vector", (lambda E: E.tensor_scalar(t7, t7, -0.5, 1.5, ALU.mult, ALU.add)),
                   reads=[blnst], writes=[blnst])
                em("vector", (lambda E: E.tensor_mul(r1, r1, t7)),
                   reads=[blnst], writes=[blnst])
                em("vector", (lambda E, r1=r1: E.tensor_scalar_mul(scr[:, :], scr[:, :], r1)),
                   reads=[blnst], writes=[bscr])
                em("vector", (lambda E: E.tensor_mul(scr[:, :], scr[:, :], sln1w)),
                   reads=[bsln], writes=[bscr])
                em("vector", (lambda E: E.tensor_add(scr[:, :], scr[:, :], sln1b)),
                   reads=[bsln], writes=[bscr])
                transpose_split(
                    scr[:, :], bscr,
                    dst_hi_mk=lambda g: yT_hi[:, g * 512:(g + 1) * 512].rearrange("p (c n) -> p c n", c=4),
                    dst_lo_mk=lambda g: yT_lo[:, g * 512:(g + 1) * 512].rearrange("p (c n) -> p c n", c=4),
                    dst_bufs=[b_yT])
                fns0, fns1, fns2 = [], [], []
                for k in range(KC):
                    lh = yT_hi[:, k * 128:(k + 1) * 128]
                    ll = yT_lo[:, k * 128:(k + 1) * 128]
                    fns0 += mm3_group(PB[0][:, :], lh, ll,
                                      wv_hi[:, k * 1024:k * 1024 + 512],
                                      wv_lo[:, k * 1024:k * 1024 + 512],
                                      first=(k == 0), last=(k == KC - 1))
                    fns1 += mm3_group(PB[1][:, :], lh, ll,
                                      wv_hi[:, k * 1024 + 512:(k + 1) * 1024],
                                      wv_lo[:, k * 1024 + 512:(k + 1) * 1024],
                                      first=(k == 0), last=(k == KC - 1))
                    fns2 += mm3_group(PB[7][:, 64:96], lh, ll,
                                      wqk_hi[:, k * 32:(k + 1) * 32],
                                      wqk_lo[:, k * 32:(k + 1) * 32],
                                      first=(k == 0), last=(k == KC - 1))
                em("tensor", fns0, reads=[b_yT, b_wv], writes=[bpb[0]])
                em("tensor", fns1, reads=[b_yT, b_wv], writes=[bpb[1]])
                em("tensor", fns2, reads=[b_yT, b_wqk], writes=[bpb[7]])

            def back1(tt):
                em("scalar", (lambda E: E.activation(out=tq[:, :], in_=PB[7][:, 64:96], func=AF.Tanh)),
                   reads=[bpb[7]], writes=[btq])
                em("scalar", (lambda E: E.activation(out=cosqk[:, :], in_=tq[:, :], func=AF.Sin,
                                                     bias=PI / 2, scale=PI / 4)),
                   reads=[btq], writes=[bcos])
                em("scalar", (lambda E: E.activation(out=sinqk[:, :], in_=tq[:, :], func=AF.Sin,
                                                     bias=0.0, scale=PI / 4)),
                   reads=[btq], writes=[bsin])
                em("vector", (lambda E: E.tensor_tensor(
                    csin[:, 0:512].rearrange("p (i h) -> p i h", i=8),
                    PB[0][:, :].rearrange("p (i h) -> p i h", i=8),
                    cosqk[:, 16:24].broadcast_to([128, 8, 64]), ALU.mult)),
                   reads=[bpb[0], bcos], writes=[bcsin])
                em("vector", (lambda E: E.tensor_tensor(
                    csin[:, 512:1024].rearrange("p (i h) -> p i h", i=8),
                    PB[1][:, :].rearrange("p (i h) -> p i h", i=8),
                    cosqk[:, 24:32].broadcast_to([128, 8, 64]), ALU.mult)),
                   reads=[bpb[1], bcos], writes=[bcsin])
                em("vector", (lambda E: E.tensor_tensor(
                    csin[:, 1024:1536].rearrange("p (i h) -> p i h", i=8),
                    PB[0][:, :].rearrange("p (i h) -> p i h", i=8),
                    sinqk[:, 16:24].broadcast_to([128, 8, 64]), ALU.mult)),
                   reads=[bpb[0], bsin], writes=[bcsin])
                em("vector", (lambda E: E.tensor_tensor(
                    csin[:, 1536:2048].rearrange("p (i h) -> p i h", i=8),
                    PB[1][:, :].rearrange("p (i h) -> p i h", i=8),
                    sinqk[:, 24:32].broadcast_to([128, 8, 64]), ALU.mult)),
                   reads=[bpb[1], bsin], writes=[bcsin])
                em("vector", (lambda E: E.tensor_copy(csin[:, 2048:2064], cosqk[:, 16:32])),
                   reads=[bcos], writes=[bcsin])
                em("vector", (lambda E: E.tensor_copy(csin[:, 2064:2080], sinqk[:, 16:32])),
                   reads=[bsin], writes=[bcsin])

            def back2(tt):
                t = t0 + tt
                xt = XRES[:, t * 1024:(t + 1) * 1024]
                groups = [(0, 512, PB[3], 3), (512, 512, PB[4], 4),
                          (1024, 512, PB[5], 5), (1536, 512, PB[6], 6),
                          (2048, 32, PB[7], 7)]
                for (off, w, pbk, pbi) in groups:
                    fns = []
                    if t > 0:
                        fns.append(lambda E, pbk=pbk, off=off, w=w: E.matmul(
                            pbk[:, 0:w], sonesr[0:1, :], carry[0:1, off:off + w],
                            start=True, stop=False))
                    fns.append(lambda E, pbk=pbk, off=off, w=w, t=t: E.matmul(
                        pbk[:, 0:w], sU[:, :], csin[:, off:off + w],
                        start=(t == 0), stop=True))
                    em("tensor", fns, reads=[bcsin, bcarry, bconsts], writes=[bpb[pbi]])
                if t < NT - 1:
                    for (off, w, pbk, pbi) in groups:
                        em("scalar", (lambda E, pbk=pbk, off=off, w=w: E.copy(
                            out=csin[96:128, off:off + w], in_=pbk[96:128, 0:w])),
                           reads=[bpb[pbi]], writes=[bcsin])
                    em("sync", (lambda E: E.dma_start(
                        out=carry[0:1, 0:2080], in_=csin[127:128, 0:2080])),
                       reads=[bcsin], writes=[bcarry], sem=dCR, inc=16)
                em("vector", (lambda E: E.tensor_tensor(dsc[:, 0:16], PB[7][:, 0:16],
                                                        cosqk[:, 0:16], ALU.mult)),
                   reads=[bpb[7], bcos], writes=[bdsc])
                em("vector", (lambda E: E.tensor_tensor(dsc[:, 16:32], PB[7][:, 16:32],
                                                        sinqk[:, 0:16], ALU.mult)),
                   reads=[bpb[7], bsin], writes=[bdsc])
                em("vector", (lambda E: E.tensor_add(dsc[:, 0:16], dsc[:, 0:16], dsc[:, 16:32])),
                   reads=[bdsc], writes=[bdsc])
                em("vector", (lambda E: E.tensor_scalar_add(dsc[:, 0:16], dsc[:, 0:16], COS_EPS)),
                   reads=[bdsc], writes=[bdsc])
                em("vector", (lambda E: E.reciprocal(dsc[:, 32:48], dsc[:, 0:16])),
                   reads=[bdsc], writes=[bdsc])
                em("vector", (lambda E: E.tensor_tensor(dsc[:, 48:64], cosqk[:, 0:16],
                                                        dsc[:, 32:48], ALU.mult)),
                   reads=[bdsc, bcos], writes=[bdsc])
                em("vector", (lambda E: E.tensor_tensor(dsc[:, 64:80], sinqk[:, 0:16],
                                                        dsc[:, 32:48], ALU.mult)),
                   reads=[bdsc, bsin], writes=[bdsc])
                for hh in range(2):
                    cu_c = PB[3 + hh]
                    cu_s = PB[5 + hh]
                    q0 = hh * 8
                    dst = csin[:, hh * 512:(hh + 1) * 512]
                    tmp = csin[:, 1024 + hh * 512:1024 + (hh + 1) * 512]
                    em("vector", (lambda E, dst=dst, cu_c=cu_c, q0=q0: E.tensor_tensor(
                        dst.rearrange("p (i h) -> p i h", i=8),
                        cu_c[:, :].rearrange("p (i h) -> p i h", i=8),
                        dsc[:, 48 + q0:48 + q0 + 8].broadcast_to([128, 8, 64]), ALU.mult)),
                       reads=[bpb[3 + hh], bdsc, bcsin], writes=[bcsin])
                    em("vector", (lambda E, tmp=tmp, cu_s=cu_s, q0=q0: E.tensor_tensor(
                        tmp.rearrange("p (i h) -> p i h", i=8),
                        cu_s[:, :].rearrange("p (i h) -> p i h", i=8),
                        dsc[:, 64 + q0:64 + q0 + 8].broadcast_to([128, 8, 64]), ALU.mult)),
                       reads=[bpb[5 + hh], bdsc, bcsin], writes=[bcsin])
                    em("vector", (lambda E, dst=dst, tmp=tmp: E.tensor_add(dst, dst, tmp)),
                       reads=[bcsin], writes=[bcsin])
                for g in range(2):
                    pb = PB[3 + g]
                    fns = []
                    for c4 in range(4):
                        c = g * 4 + c4
                        fns.append(lambda E, c=c, c4=c4, pb=pb: E.transpose(
                            pb[:, c4 * 128:(c4 + 1) * 128],
                            csin[:, c * 128:(c + 1) * 128], sI[:, :]))
                    em("tensor", fns, reads=[bcsin, bconsts], writes=[bpb[3 + g]])
                    em("scalar", (lambda E, pb=pb, g=g: E.copy(
                        out=hT_hi[:, g * 512:(g + 1) * 512], in_=pb[:, :])),
                       reads=[bpb[3 + g]], writes=[bcsin])
                    em("vector", (lambda E, pb=pb, g=g: E.tensor_tensor(
                        hT_lo[:, g * 512:(g + 1) * 512], pb[:, :],
                        hT_hi[:, g * 512:(g + 1) * 512], ALU.subtract)),
                       reads=[bpb[3 + g], bcsin], writes=[bcsin])
                for hh in range(2):
                    fns = []
                    for k in range(KC):
                        lh = hT_hi[:, k * 128:(k + 1) * 128]
                        ll = hT_lo[:, k * 128:(k + 1) * 128]
                        fns += mm3_group(PB[5 + hh][:, :], lh, ll,
                                         wo_hi[:, k * 1024 + hh * 512:k * 1024 + (hh + 1) * 512],
                                         wo_lo[:, k * 1024 + hh * 512:k * 1024 + (hh + 1) * 512],
                                         first=(k == 0), last=(k == KC - 1))
                    em("tensor", fns, reads=[bcsin, b_wo], writes=[bpb[5 + hh]])
            def back2b(tt):
                t = t0 + tt
                xt = XRES[:, t * 1024:(t + 1) * 1024]
                em("vector", (lambda E, xt=xt, t=t: E.tensor_scalar(
                    scr[:, :], xt, murs[:, 2 * t:2 * t + 1], murs[:, 2 * t + 1:2 * t + 2],
                    ALU.subtract, ALU.mult)),
                   reads=[bx[t], bmurs], writes=[bscr])
                em("vector", (lambda E: E.tensor_add(scr[:, 0:512], scr[:, 0:512], PB[5][:, :])),
                   reads=[bpb[5]], writes=[bscr])
                em("vector", (lambda E: E.tensor_add(scr[:, 512:1024], scr[:, 512:1024], PB[6][:, :])),
                   reads=[bpb[6]], writes=[bscr])
                em("vector", (lambda E, xt=xt: E.tensor_add(xt, xt, scr[:, :])),
                   reads=[bscr], writes=[bx[t]])
                ln_stats(scr[:, :], [bscr], 4, 5, blnst,
                         sq=csin[:, 1024:2048], sq_buf=bcsin)
                normalize(scr[:, :], bscr, scr[:, :], [bscr], 4, 5, blnst)
                em("vector", (lambda E: E.tensor_mul(scr[:, :], scr[:, :], sln2w)),
                   reads=[bsln], writes=[bscr])
                em("vector", (lambda E: E.tensor_add(scr[:, :], scr[:, :], sln2b)),
                   reads=[bsln], writes=[bscr])

            def back2c(tt):
                transpose_split(
                    scr[:, :], bscr,
                    dst_hi_mk=lambda g, tt=tt: arnb_y2_hi_r[:, g * 4:(g + 1) * 4, tt * 128:(tt + 1) * 128],
                    dst_lo_mk=lambda g, tt=tt: arnb_y2_lo_r[:, g * 4:(g + 1) * 4, tt * 128:(tt + 1) * 128],
                    dst_bufs=[b_y2T], banks=(3, 4))

            if do_att and att_stop >= 1:
                front(0)
                back1(0)
                for tt in range(tiles_per_seg):
                    if tt > 0:
                        back2c(tt - 1)
                    back2(tt)
                    if tt + 1 < tiles_per_seg:
                        front(tt + 1)
                    back2b(tt)
                    if tt + 1 < tiles_per_seg:
                        back1(tt + 1)
                back2c(tiles_per_seg - 1)

            q_tok = qtiles * 128
            assert 64 * q_tok <= 16384
            h1_hi = ARNB[:, 0:32 * q_tok]
            h1_lo = ARNB[:, 32 * q_tok:64 * q_tok]
            b_h1 = brole[2]
            toks_w = arena_turnover([b_wv, b_wo, b_wqk])
            w1stg = stg
            b_w1stg = barnc[4]
            stg2 = arnc_f32[:, 4096:4608]
            b_stg2 = barnc[7]
            stgs = (w1stg, stg2)
            b_stgs = (b_w1stg, b_stg2)

            def w1_hi(sl):
                return ARNC[:, sl * 2048: sl * 2048 + 1024]

            def w1_lo(sl):
                return ARNC[:, sl * 2048 + 1024: sl * 2048 + 2048]


            b_w1 = [barnc[0], barnc[1]]

            def w2_hi(sl):
                return ARNC[:, 4096 + sl * 2048: 4096 + sl * 2048 + 1024]

            def w2_lo(sl):
                return ARNC[:, 4096 + sl * 2048 + 1024: 4096 + sl * 2048 + 2048]

            b_w2 = [barnc[2], barnc[6]]
            w2stg = stg
            b_w2stg = barnc[4]
            em("sync", (lambda E: E.dma_start(out=sb2bc, in_=b2bc_h[:, :])),
               writes=[bsln], sem=dBC, inc=16)
            for b in (b_w1[0], b_w1[1], b_w2[0], b_w2[1], b_h1, b_w1stg, b_w2stg):
                b.r.extend(toks_w)

            nquart = tiles_per_seg // qtiles
            for q in (range(nquart) if (do_mlp and do_att) else []):
                qt0 = t0 + q * qtiles
                seg_off = q * qtiles * 128
                def load_w1(o, sl):
                    for wv_w, q_eng, q_sem in ((0, "sync", dW1), (1, "scalar", dW1s)):
                        em(q_eng, (lambda E, o=o, wv_w=wv_w: E.dma_start(
                            out=stgs[wv_w], in_=w1_h[o, :, wv_w * 512:(wv_w + 1) * 512])),
                           writes=[b_stgs[wv_w]], sem=q_sem, inc=16)
                        em("vector", (lambda E, sl=sl, wv_w=wv_w: E.tensor_copy(
                            w1_hi(sl)[:, wv_w * 512:(wv_w + 1) * 512], stgs[wv_w])),
                           reads=[b_stgs[wv_w]], writes=[b_w1[sl]])
                        em("vector", (lambda E, sl=sl, wv_w=wv_w: E.tensor_tensor(
                            w1_lo(sl)[:, wv_w * 512:(wv_w + 1) * 512], stgs[wv_w],
                            w1_hi(sl)[:, wv_w * 512:(wv_w + 1) * 512], ALU.subtract)),
                           reads=[b_stgs[wv_w], b_w1[sl]], writes=[b_w1[sl], b_stgs[wv_w]])

                load_w1(0, 0)
                for o in range(32):
                    sl = o % 2
                    if o + 1 < 32:
                        load_w1(o + 1, 1 - sl)
                    pb = PB[o % 4]
                    pbuf = bpb[o % 4]
                    fns = []
                    for k in range(KC):
                        rh = y2T_hi(k)[:, seg_off:seg_off + q_tok]
                        rl = y2T_lo(k)[:, seg_off:seg_off + q_tok]
                        fns += mm3_group(pb[:, 0:q_tok],
                                         w1_hi(sl)[:, k * 128:(k + 1) * 128],
                                         w1_lo(sl)[:, k * 128:(k + 1) * 128],
                                         rh, rl, first=(k == 0), last=(k == KC - 1))
                    em("tensor", fns, reads=[b_w1[sl], b_y2T], writes=[pbuf])
                    em("scalar", (lambda E, pb=pb, o=o: E.activation(
                        out=scrF[:, 0:q_tok], in_=pb[:, 0:q_tok], func=AF.Gelu_apprx_tanh,
                        bias=sb1c[:, o:o + 1])),
                       reads=[pbuf, bsb], writes=[bscrF])
                    em("scalar", (lambda E, o=o: E.copy(
                        out=h1_hi[:, o * q_tok:(o + 1) * q_tok], in_=scrF[:, 0:q_tok])),
                       reads=[bscrF], writes=[b_h1])
                    em("vector", (lambda E, o=o: E.tensor_tensor(
                        h1_lo[:, o * q_tok:(o + 1) * q_tok], scrF[:, 0:q_tok],
                        h1_hi[:, o * q_tok:(o + 1) * q_tok], ALU.subtract)),
                       reads=[bscrF, b_h1], writes=[b_h1])
                def load_w2(k, sl):
                    for wv_w, q_eng, q_sem in ((0, "sync", dW2), (1, "scalar", dW2s)):
                        em(q_eng, (lambda E, k=k, wv_w=wv_w: E.dma_start(
                            out=stgs[wv_w], in_=w2_h[k * 128:(k + 1) * 128, wv_w * 512:(wv_w + 1) * 512])),
                           writes=[b_stgs[wv_w]], sem=q_sem, inc=16)
                        em("vector", (lambda E, sl=sl, wv_w=wv_w: E.tensor_copy(
                            w2_hi(sl)[:, wv_w * 512:(wv_w + 1) * 512], stgs[wv_w])),
                           reads=[b_stgs[wv_w]], writes=[b_w2[sl]])
                        em("vector", (lambda E, sl=sl, wv_w=wv_w: E.tensor_tensor(
                            w2_lo(sl)[:, wv_w * 512:(wv_w + 1) * 512], stgs[wv_w],
                            w2_hi(sl)[:, wv_w * 512:(wv_w + 1) * 512], ALU.subtract)),
                           reads=[b_stgs[wv_w], b_w2[sl]], writes=[b_w2[sl], b_stgs[wv_w]])

                load_w2(0, 0)
                for k in range(32):
                    sl = k % 2
                    if k + 1 < 32:
                        load_w2(k + 1, 1 - sl)
                    for ti in range(qtiles):
                        lh = h1_hi[:, k * q_tok + ti * 128: k * q_tok + (ti + 1) * 128]
                        ll = h1_lo[:, k * q_tok + ti * 128: k * q_tok + (ti + 1) * 128]
                        for hh in range(2):
                            pb = PB[ti * 2 + hh]
                            fns = mm3_group(pb[:, :], lh, ll,
                                            w2_hi(sl)[:, hh * 512:(hh + 1) * 512],
                                            w2_lo(sl)[:, hh * 512:(hh + 1) * 512],
                                            first=(k == 0), last=(k == 31))
                            em("tensor", fns, reads=[b_h1, b_w2[sl]], writes=[bpb[ti * 2 + hh]])
                for ti in range(qtiles):
                    t = qt0 + ti
                    xt = XRES[:, t * 1024:(t + 1) * 1024]
                    for hh in range(2):
                        em("vector", (lambda E, xt=xt, ti=ti, hh=hh: E.tensor_add(
                            xt[:, hh * 512:(hh + 1) * 512], xt[:, hh * 512:(hh + 1) * 512],
                            PB[ti * 2 + hh][:, :])),
                           reads=[bpb[ti * 2 + hh]], writes=[bx[t]])
                    em("vector", (lambda E, xt=xt: E.tensor_add(xt, xt, sb2bc)),
                       reads=[bsln], writes=[bx[t]])

            toks = arena_turnover([b_wv, b_wo, b_wqk, b_y2T, b_h1, b_stg,
                                   b_w1[0], b_w1[1], b_w2[0], b_w2[1], b_w1stg, b_w2stg])
            for b in barnb + barnc + brole:
                b.r.extend([tk for tk in toks if tk])

        rowbufs = [bcarry] + list(bbrow) + list(bbrowh) + list(bbrowl)
        if "f1" in phases:
            ffn_phase(0)
            _bp2_handoff()
        toks = arena_turnover(barnb + barnc + brole + rowbufs)
        for lst in (barnb, barnc, brole, rowbufs):
            for b in lst:
                b.r.extend(toks)
        if "att" in phases or "mlp" in phases:
            for seg in range(nseg):
                attn_mlp_segment(seg)
        toks = arena_turnover(barnb + barnc + brole + rowbufs)
        for lst in (barnb, barnc, brole, rowbufs):
            for b in lst:
                b.r.extend(toks)
        if "f2" in phases:
            ffn_phase(1)
            _bp2_handoff()

        store_toks = []
        for t in range(NT):
            tok = em("sync", (lambda E, t=t: E.dma_start(
                out=y_h[t * 128:(t + 1) * 128, :], in_=XRES[:, t * 1024:(t + 1) * 1024])),
                reads=[bx[t]], sem=dST, inc=16)
            store_toks.append(tok)
        em("gpsimd", (lambda E: E.engine_nop() if hasattr(E, "engine_nop") else E.memset(tq[:, 0:1], 0.0)),
           extra=[store_toks[-1]])

        with nc.Block() as block:
            @block.sync
            def _(e):
                for cmd in S.q["sync"]:
                    cmd(e)

            @block.gpsimd
            def _(e):
                for cmd in S.q["gpsimd"]:
                    cmd(e)

            @block.tensor
            def _(e):
                for cmd in S.q["tensor"]:
                    cmd(e)

            @block.scalar
            def _(e):
                for cmd in S.q["scalar"]:
                    cmd(e)

            @block.vector
            def _(e):
                for cmd in S.q["vector"]:
                    cmd(e)

    return nc


def _prep(inputs):
    f32 = np.float32
    import ml_dtypes
    bf16 = ml_dtypes.bfloat16
    x = np.asarray(inputs["L0_attn_x"], f32)
    B, T, _ = x.shape
    WQ = np.asarray(inputs["L0_attn_W_Q"], f32)
    WK = np.asarray(inputs["L0_attn_W_K"], f32)
    WV = np.asarray(inputs["L0_attn_W_V"], f32)
    WO = np.asarray(inputs["L0_attn_W_O"], f32)
    W1 = np.asarray(inputs["L0_attn_W1"], f32)
    b1 = np.asarray(inputs["L0_attn_b1"], f32)
    W2 = np.asarray(inputs["L0_attn_W2"], f32)
    b2 = np.asarray(inputs["L0_attn_b2"], f32)
    gw = np.asarray(inputs["L0_ffn_gate_w"], f32)
    gb = np.asarray(inputs["L0_ffn_gate_b"], f32)
    ew = np.asarray(inputs["L0_ffn_expert_w"], f32)
    eb = np.asarray(inputs["L0_ffn_expert_b"], f32)
    ln1w = np.asarray(inputs["L0_attn_ln1_w"], f32)
    ln1b = np.asarray(inputs["L0_attn_ln1_b"], f32)
    ln2w = np.asarray(inputs["L0_attn_ln2_w"], f32)
    ln2b = np.asarray(inputs["L0_attn_ln2_b"], f32)

    C = np.ascontiguousarray
    com = {}
    com["wv"] = C(np.transpose(WV, (1, 0, 2)).reshape(D, NH * DH))
    com["wqk"] = C(np.concatenate([WQ[:, :, 0].T, WK[:, :, 0].T], axis=1))
    com["wo"] = C(np.transpose(WO, (2, 1, 0)).reshape(NH * DH, D))
    com["w1"] = C(W1.reshape(8, 128, 32, 128).transpose(2, 1, 0, 3).reshape(32, 128, 1024))
    com["w2"] = C(W2)
    com["b1c"] = C(b1.reshape(32, 128).T)
    com["b2bc"] = C(np.broadcast_to(b2[None, :], (128, D)))
    expT = np.transpose(ew, (0, 2, 1))
    com["expw"] = C(expT.reshape(NE, 8, 128, 2, 512).transpose(0, 3, 2, 1, 4).reshape(NE, 2, 128, 4096))
    com["expb"] = C(eb)
    com["gw"] = C(gw.T.reshape(8, 128, 8).transpose(1, 0, 2).reshape(128, 64))
    com["gb"] = C(gb.reshape(1, 8))
    com["ln1w"] = C(np.broadcast_to(ln1w[None, :], (128, D)))
    com["ln1b"] = C(np.broadcast_to(ln1b[None, :], (128, D)))
    com["ln2w"] = C(np.broadcast_to(ln2w[None, :], (128, D)))
    com["ln2b"] = C(np.broadcast_to(ln2b[None, :], (128, D)))
    total = float(B * T * 2)
    CONST_VALS = [0.0, 1.0, PI / 2, PI / 4, LN_EPS, COS_EPS, 1.0 / D, 0.5 / total]
    com["consts"] = C(np.tile(np.asarray(CONST_VALS, f32)[None, :], (128, 1)))
    com["ones_col"] = np.ones((128, 1), f32)
    com["ones_col_b"] = np.ones((128, 1), bf16)
    com["ones_row"] = np.ones((1, 128), f32)
    com["ones2"] = np.ones((2, 128), bf16)
    com["u128"] = C(np.triu(np.ones((128, 128), f32)))
    com["ident"] = np.eye(128, dtype=f32)
    shards = [C(x[i]) for i in range(B)]
    return com, shards, B, T


def kernel(**inputs):
    com, shards, B, T = _prep(inputs)
    nc = build_nc(T, n_cores=B)
    in_maps = [dict(com, x=s) for s in shards]
    res = run_bass_kernel_spmd(nc, in_maps, core_ids=list(range(B)))
    out = np.stack([np.asarray(res.results[i]["y"]) for i in range(B)], axis=0)
    return out.astype(np.float32)
